# version 17
# speedup vs baseline: 1.1319x; 1.0089x over previous
import numpy as np
import concourse.bass as bass
import concourse.mybir as mybir
from concourse.ap import AP
from concourse.bass_utils import run_bass_kernel_spmd

F32 = mybir.dt.float32
BF16 = mybir.dt.bfloat16
AF = mybir.ActivationFunctionType
ALU = mybir.AluOpType
AX = mybir.AxisListType

N_CORES = 8
D = 1024
DL = 256
INNER = 4096
TOK = 32768 // N_CORES
TILE = 512
NT = TOK // TILE
NTC = TILE // 128
EPS = 1e-5
C23 = 1.5 * 2.0 ** 23
C52 = 1.5 * 2.0 ** 52

ENGS = ("sync", "tensor", "vector", "scalar", "gpsimd")


def build_nc():
    nc = bass.Bass("TRN2", target_bir_lowering=False)

    x_h = nc.dram_tensor("x", [TOK, D], F32, kind="ExternalInput").ap()
    wq_h = nc.dram_tensor("wq", [D, D], F32, kind="ExternalInput").ap()
    bq_h = nc.dram_tensor("bq", [D], F32, kind="ExternalInput").ap()
    wd_h = nc.dram_tensor("wd", [D, DL], F32, kind="ExternalInput").ap()
    bd_h = nc.dram_tensor("bd", [DL], F32, kind="ExternalInput").ap()
    wu_h = nc.dram_tensor("wu", [DL, 2 * D], F32, kind="ExternalInput").ap()
    bu_h = nc.dram_tensor("bu", [2 * D], F32, kind="ExternalInput").ap()
    wo_h = nc.dram_tensor("wo", [D, D], F32, kind="ExternalInput").ap()
    bo_h = nc.dram_tensor("bo", [D], F32, kind="ExternalInput").ap()
    wg_h = nc.dram_tensor("wg", [INNER, D], F32, kind="ExternalInput").ap()
    wup_h = nc.dram_tensor("wup", [INNER, D], F32, kind="ExternalInput").ap()
    wdn_h = nc.dram_tensor("wdn", [D, INNER], F32, kind="ExternalInput").ap()
    wr_h = nc.dram_tensor("wr", [64, TOK], F32, kind="ExternalInput").ap()
    idn_h = nc.dram_tensor("idn", [128, 128], F32, kind="ExternalInput").ap()
    idnb_h = nc.dram_tensor("idnb", [128, 128], BF16, kind="ExternalInput").ap()
    ones_h = nc.dram_tensor("ones", [128, 128], F32, kind="ExternalInput").ap()
    emask_h = nc.dram_tensor("emask", [128, 512], F32, kind="ExternalInput").ap()
    out_h = nc.dram_tensor("out", [TOK, D], F32, kind="ExternalOutput").ap()

    tgT_h = nc.dram_tensor("tgT", [D, INNER], BF16, kind="Internal").ap()
    tuT_h = nc.dram_tensor("tuT", [D, INNER], BF16, kind="Internal").ap()
    tdT_h = nc.dram_tensor("tdT", [INNER, D], BF16, kind="Internal").ap()
    gp_h = nc.dram_tensor("gpscr", [TILE, INNER], F32, kind="Internal").ap()

    from contextlib import ExitStack
    stack = ExitStack()
    en = stack.enter_context

    ident = en(nc.sbuf_tensor("ident", [128, 128], F32))
    identb = en(nc.sbuf_tensor("identb", [128, 128], BF16))
    ones = en(nc.sbuf_tensor("onest", [128, 128], F32))
    mg = en(nc.sbuf_tensor("mg", [128, 8], F32))
    bqT = en(nc.sbuf_tensor("bqT", [128, 8], F32))
    bdT = en(nc.sbuf_tensor("bdT", [128, 2], F32))
    buT = en(nc.sbuf_tensor("buT", [128, 16], F32))
    wmean = en(nc.sbuf_tensor("wmean", [128, 4], F32))
    swcol = en(nc.sbuf_tensor("swcol", [128, 4], F32))
    lnst = en(nc.sbuf_tensor("lnst", [128, 32], F32))
    qsc = en(nc.sbuf_tensor("qsc", [128, 16], F32))
    redc = en(nc.sbuf_tensor("redc", [128, 36], F32))
    xring = [en(nc.sbuf_tensor(f"xr{i}", [128, D], F32)) for i in range(2)]
    kqst = en(nc.sbuf_tensor("kqst", [128, D], BF16))
    kqT = en(nc.sbuf_tensor("kqT", [128, 8 * TILE], BF16))
    wring = [en(nc.sbuf_tensor(f"wring{i}", [128, D], F32)) for i in range(4)]
    emask = en(nc.sbuf_tensor("emask_s", [128, 512], F32))
    dnst = en(nc.sbuf_tensor("dnst", [128, 16], F32))
    bo_row = en(nc.sbuf_tensor("borow", [1, D], F32))
    wdel = en(nc.sbuf_tensor("wdel", [64, 512], F32))

    ARW = 26624
    ar = en(nc.sbuf_tensor("arena", [128, ARW], F32))
    arh = ar

    def A_(off, w):
        assert off + w <= ARW, (off, w)
        return ar[:, off:off + w]

    kvT2 = A_(0, 8192)
    qT2 = A_(8192, 8192)
    OF_QT = 8192
    OF_AVSB = 8192 + 4096
    xnT = A_(16384, 4096)
    fin1 = A_(16384, 4096)
    latT = A_(20480, 1024)
    xn = A_(20480, 1024)
    v3 = A_(21504, 2048)
    avch = [A_(23552 + i * 512, 512) for i in range(2)]
    esw = [A_(24576 + i * 512, 512) for i in range(2)]
    ediag = [A_(25600 + i * 512, 512) for i in range(2)]

    gprd = A_(0, 4096)
    gpstg = A_(4096, 2048)
    silustg = A_(6144, 1024)
    a2stg = A_(7168, 1024)
    fin2 = [A_(4096 + i * 1024, 1024) for i in range(2)]
    wstg = [A_(8192 + i * 2048, 2048) for i in range(2)]


    ABW = 24576
    arB = en(nc.sbuf_tensor("arenab", [128, ABW], BF16))

    def B_(off, w):
        assert off + w <= ABW, (off, w)
        return arB[:, off:off + w]

    kq2T = B_(0, 8192)
    gbufA = [B_(8192, 4096), B_(12288, 4096)]
    gbufB = [B_(16384, 4096), B_(20480, 4096)]
    tdring = [B_(16384 + i * 2048, 2048) for i in range(2)]
    k2 = B_(20480, 4096)
    tstg = B_(0, 4096)
    ttout = B_(4096, 4096)

    PD = [en(nc.psum_tensor(f"pd{i}", [128, 1024], F32)) for i in range(3)]
    PDB = en(nc.psum_tensor("pdb", [128, 2048], BF16))

    def BK(i):
        return PD[i // 2][:, 512 * (i % 2):512 * (i % 2) + 512]

    def BKB(h):
        return PDB[:, 1024 * h:1024 * (h + 1)]

    sems = {}
    DMASEMS = ("LDC", "LDS0", "LDS1", "LDX0", "LDX1",
               "LDW0", "LDW1", "LDW2", "LDW3",
               "LDG0", "LDG1", "LDD0", "LDD1", "LDGP", "LDGQ", "LDR",
               "OUTS", "OUTG0", "OUTG1", "OUTG2", "OUTG3",
               "OUTO0", "OUTO1", "SH0", "SH1")
    for name in DMASEMS + ("TE", "V", "S", "GP"):
        sems[name] = en(nc.semaphore(name))

    block = en(nc.Block())
    steps = {e: [] for e in ENGS}
    cnt = {n: 0 for n in sems}

    def W(eng, sem, v):
        assert v <= cnt[sem], (eng, sem, v, cnt[sem])
        if v <= 0:
            return

        def f(E, s=sem, vv=v):
            E.wait_ge(sems[s], vv)
            return None
        steps[eng].append(f)

    def dma(eng, sem, out_ap, in_ap, slow=False):
        cnt[sem] += 16

        def f(E, o=out_ap, i=in_ap, s=sem, sl=slow):
            if sl:
                E.dma_start(out=o, in_=i, allow_slow_non_contiguous=True).then_inc(sems[s], 16)
            else:
                E.dma_start(out=o, in_=i).then_inc(sems[s], 16)
        steps[eng].append(f)
        return cnt[sem]

    def mark(eng, sem):
        cnt[sem] += 1
        prev = steps[eng].pop() if steps[eng] else None

        def g(E, p=prev, s=sem):
            inst = p(E) if p is not None else None
            if inst is None:
                E.drain().then_inc(sems[s], 1)
            else:
                inst.then_inc(sems[s], 1)
        steps[eng].append(g)
        return cnt[sem]

    def op(eng, fn):
        steps[eng].append(fn)

    dma("sync", "LDC", ident[:, :], idn_h)
    dma("sync", "LDC", identb[:, :], idnb_h)
    dma("sync", "LDC", ones[:, :], ones_h)
    for c in range(8):
        dma("sync", "LDC", bqT[:, c:c + 1], bq_h[128 * c:128 * (c + 1)].rearrange("(p a) -> p a", a=1))
    for c in range(2):
        dma("sync", "LDC", bdT[:, c:c + 1], bd_h[128 * c:128 * (c + 1)].rearrange("(p a) -> p a", a=1))
    for c in range(16):
        dma("sync", "LDC", buT[:, c:c + 1], bu_h[128 * c:128 * (c + 1)].rearrange("(p a) -> p a", a=1))
    dma("sync", "LDC", bo_row[0:1, :], bo_h.rearrange("(a d) -> a d", a=1))
    dma("sync", "LDC", emask[:, :], emask_h)
    ld_consts = cnt["LDC"]

    def mkmagic(V):
        ts = V.tensor_scalar
        ts(mg[:, 0:1], ident[:, 0:1], 0.0, C52, ALU.mult, ALU.add)
        V.drain()
        ts(mg[:, 1:2], mg[:, 0:1], 1.0, -C52, ALU.add, ALU.add)
        ts(mg[:, 2:3], mg[:, 0:1], -C23, None, ALU.add)
        V.drain()
        V.tensor_mul(mg[:, 3:4], mg[:, 2:3], mg[:, 1:2])
        V.drain()
        ts(mg[:, 4:5], mg[:, 3:4], C23, None, ALU.add)
        V.drain()
        return ts(mg[:, 5:6], mg[:, 4:5], -1.0, None, ALU.mult)
    W("vector", "LDC", ld_consts)
    op("vector", mkmagic)
    m_magic = mark("vector", "V")

    tjobs = [(wg_h, INNER, D, tgT_h, 0), (wup_h, INNER, D, tuT_h, 1), (wdn_h, D, INNER, tdT_h, 2)]
    for (whh, R, C0, scrT, mi) in tjobs:
        nchunk = R // 128
        W("sync", "V", cnt["V"])
        W("sync", "S", cnt["S"])
        lastm = {}
        GRP = 2048 // C0 if C0 <= 2048 else 1
        nr = nchunk // GRP
        for r in range(nr):
            if C0 <= 2048:
                stg = wstg[r % 2][:, 0:GRP * C0]
                if r >= 2:
                    W("sync", "V", lastm[r - 2])
                src = whh.rearrange("(c p) w -> p c w", p=128)[:, GRP * r:GRP * (r + 1), :]
                dst = stg.rearrange("p (c w) -> p c w", w=C0)
            else:
                stg = gprd[:, 0:C0]
                if r >= 1:
                    W("sync", "V", lastm[r - 1])
                src = whh[128 * r:128 * (r + 1), :]
                dst = stg
            lv = dma("sync", "LDS0" if (r % 2 == 0 or C0 > 2048) else "LDS1", dst, src)
            W("vector", "LDS0" if (r % 2 == 0 or C0 > 2048) else "LDS1", lv)

            def fabs(V, stg=stg, r=r):
                return V.tensor_reduce(redc[:, r:r + 1], stg, AX.X, ALU.add,
                                       apply_absolute_value=True)
            op("vector", fabs)
            lastm[r] = mark("vector", "V")

        def fcomb(V, nr=nr):
            V.drain()
            return V.tensor_reduce(redc[:, 33:34], redc[:, 0:nr], AX.X, ALU.add)
        op("vector", fcomb)
        mv = mark("vector", "V")
        W("tensor", "V", mv)

        def fmm(T):
            return T.matmul(BK(0)[:, 0:1], ones[:, :], redc[:, 33:34], start=True, stop=True)
        op("tensor", fmm)
        mt = mark("tensor", "TE")
        W("vector", "TE", mt)
        NEL = float(R * C0)

        def fscale(V, NEL=NEL, mi=mi):
            V.tensor_scalar(redc[:, 34:35], BK(0)[:, 0:1], 1.0 / NEL, 1e-5, ALU.mult, ALU.max)
            V.drain()
            V.tensor_copy(wmean[:, mi:mi + 1], redc[:, 34:35])
            return V.reciprocal(swcol[:, mi:mi + 1], redc[:, 34:35])
        op("vector", fscale)
        mv_s = mark("vector", "V")
        W("scalar", "V", mv_s)
        W("sync", "V", mv_s)
        tpb = 0
        lastq = {}
        CW = GRP * C0
        for r in range(nr):
            if C0 <= 2048:
                stg = wstg[r % 2][:, 0:CW]
                if r >= 2:
                    W("sync", "S", lastq[r - 2])
                else:
                    W("sync", "V", mv_s)
                src = whh.rearrange("(c p) w -> p c w", p=128)[:, GRP * r:GRP * (r + 1), :]
                dst = stg.rearrange("p (c w) -> p c w", w=C0)
            else:
                stg = gprd[:, 0:C0]
                if r >= 1:
                    W("sync", "S", lastq[r - 1])
                else:
                    W("sync", "V", mv_s)
                src = whh[128 * r:128 * (r + 1), :]
                dst = stg
            lv = dma("sync", "LDS0" if (r % 2 == 0 or C0 > 2048) else "LDS1", dst, src)
            W("scalar", "LDS0" if (r % 2 == 0 or C0 > 2048) else "LDS1", lv)

            def fa(S, stg=stg, CW=CW, mi=mi):
                return S.activation(fin1[:, 0:CW], stg, AF.Identity, bias=mg[:, 4:5],
                                    scale=swcol[:, mi:mi + 1])
            op("scalar", fa)
            lastq[r] = mark("scalar", "S")
            W("vector", "S", lastq[r])
            W("vector", "OUTS", cnt["OUTS"])
            W("vector", "TE", cnt["TE"])

            def fq(V, CW=CW):
                V.tensor_scalar(tstg[:, 0:CW], fin1[:, 0:CW], mg[:, 5:6], -1.0, ALU.add, ALU.max)
                V.drain()
                return V.tensor_scalar(tstg[:, 0:CW], tstg[:, 0:CW], 1.0, None, ALU.min)
            op("vector", fq)
            mvq = mark("vector", "V")
            W("tensor", "V", mvq)
            nblk = CW // 128
            for wv in range(0, nblk, 16):
                nb = min(16, nblk - wv)

                def ftr(T, wv=wv, nb=nb):
                    for b in range(nb):
                        rr = T.transpose(PDB[:, 128 * b:128 * (b + 1)],
                                         tstg[:, 128 * (wv + b):128 * (wv + b + 1)], identb[:, :])
                    return rr
                op("tensor", ftr)
                mtt = mark("tensor", "TE")
                W("vector", "TE", mtt)

                def fev(V, wv=wv, nb=nb):
                    return V.tensor_copy(ttout[:, 128 * wv:128 * (wv + nb)], PDB[:, 0:128 * nb])
                op("vector", fev)
                mve = mark("vector", "V")
                W("tensor", "V", mve)
            W("scalar", "V", cnt["V"])
            nbc = C0 // 128
            for g in range(GRP):
                ch = GRP * r + g
                dma("scalar", "OUTS",
                    scrT.rearrange("(b p) c -> p b c", p=128)[:, 0:nbc, 128 * ch:128 * (ch + 1)],
                    ttout[:, C0 * g:C0 * (g + 1)].rearrange("p (b c) -> p b c", c=128))
    startup_out = cnt["OUTS"]

    marks = {}
    xe_last = {0: 0, 1: 0}
    wring_last = [0] * 4
    W("sync", "OUTS", startup_out)

    for it in range(NT):
        if it > 0:
            for e in ("tensor", "vector", "scalar", "gpsimd"):
                W(e, "TE", marks[("tile_te", it - 1)])
                W(e, "V", marks[("tile_v", it - 1)])
                W(e, "S", marks[("tile_s", it - 1)])
                for ds in marks[("tile_dma", it - 1)]:
                    W(e, ds, marks[("tile_dma", it - 1)][ds])
        else:
            for e in ("tensor", "scalar"):
                W(e, "V", m_magic)

        m_fqk_prev = 0
        for j in range(NTC):
            xr = xring[j % 2]
            xsem = "LDX0" if j % 2 == 0 else "LDX1"
            W("sync", "V", xe_last[j % 2])
            lv = dma("sync", xsem, xr[:, :], x_h[TILE * it + 128 * j: TILE * it + 128 * (j + 1), :])
            W("vector", xsem, lv)
            W("scalar", xsem, lv)
            if m_fqk_prev:
                W("scalar", "V", m_fqk_prev)

            def fsum(V, xr=xr):
                return V.tensor_reduce(lnst[:, 0:1], xr[:, :], AX.X, ALU.add)
            op("vector", fsum)

            def fsq(S, xr=xr):
                return S.activation(gprd[:, 0:D], xr[:, :], AF.Square, accum_out=lnst[:, 1:2])
            op("scalar", fsq)
            ms = mark("scalar", "S")
            W("vector", "S", ms)

            def fstat(V):
                ts = V.tensor_scalar
                V.drain()
                ts(lnst[:, 2:3], lnst[:, 0:1], 1.0 / D, None, ALU.mult)
                ts(lnst[:, 3:4], lnst[:, 1:2], 1.0 / D, None, ALU.mult)
                V.drain()
                V.tensor_mul(lnst[:, 4:5], lnst[:, 2:3], lnst[:, 2:3])
                V.drain()
                V.scalar_tensor_tensor(lnst[:, 4:5], lnst[:, 3:4], 1.0, lnst[:, 4:5],
                                       ALU.bypass, ALU.subtract)
                V.drain()
                return ts(lnst[:, 4:5], lnst[:, 4:5], EPS, None, ALU.add)
            op("vector", fstat)
            mv = mark("vector", "V")
            W("scalar", "V", mv)

            def fsqrt(S):
                S.activation(lnst[:, 5:6], lnst[:, 4:5], AF.Ln)
                S.drain()
                return S.activation(lnst[:, 6:7], lnst[:, 5:6], AF.Exp, scale=-0.5)
            op("scalar", fsqrt)
            ms = mark("scalar", "S")
            W("vector", "S", ms)

            def fxn(V, j=j, xr=xr):
                V.tensor_scalar(xn[:, :], xr[:, :], lnst[:, 2:3], lnst[:, 6:7],
                                ALU.subtract, ALU.mult)
                V.drain()
                V.tensor_reduce(lnst[:, 8 + j:9 + j], xn[:, :], AX.X, ALU.max,
                                apply_absolute_value=True)
                V.drain()
                V.tensor_scalar(lnst[:, 12 + j:13 + j], lnst[:, 8 + j:9 + j], 1e-5, None, ALU.max)
                V.drain()
                V.reciprocal(lnst[:, 16 + j:17 + j], lnst[:, 12 + j:13 + j])
                V.drain()
                return V.tensor_scalar(lnst[:, 16 + j:17 + j], lnst[:, 16 + j:17 + j], 127.0,
                                       None, ALU.mult)
            op("vector", fxn)
            xe_last[j % 2] = mark("vector", "V")
            W("scalar", "V", xe_last[j % 2])

            def fqa(S, j=j):
                return S.activation(gprd[:, 0:D], xn[:, :], AF.Identity,
                                    bias=mg[:, 4:5], scale=lnst[:, 16 + j:17 + j])
            op("scalar", fqa)
            ms = mark("scalar", "S")
            W("vector", "S", ms)

            def fqk(V):
                return V.tensor_scalar(kqst[:, :], gprd[:, 0:D], mg[:, 5:6], None, ALU.add)
            op("vector", fqk)
            mv = mark("vector", "V")
            m_fqk_prev = mv
            W("tensor", "V", mv)
            for half in range(2):
                bank = BKB(half)

                def ftr(T, half=half, bank=bank):
                    for b in range(4):
                        r = T.transpose(bank[:, 128 * b:128 * (b + 1)],
                                        kqst[:, 128 * (4 * half + b):128 * (4 * half + b + 1)],
                                        identb[:, :])
                    return r
                op("tensor", ftr)
                mt = mark("tensor", "TE")
                W("vector", "TE", mt)

                def fev(V, j=j, half=half, bank=bank):
                    dst = kqT.rearrange("p (i t) -> p i t", t=TILE)[
                        :, 4 * half:4 * half + 4, 128 * j:128 * (j + 1)]
                    return V.tensor_copy(dst, bank[:, 0:512].rearrange("p (i t) -> p i t", t=128))
                op("vector", fev)
                mv = mark("vector", "V")
                W("tensor", "V", mv)
            for half in range(2):
                bank = BK(4 + half)

                def ftr2(T, half=half, bank=bank):
                    for b in range(4):
                        q = 4 * half + b
                        r = T.transpose(bank[:, 128 * b:128 * (b + 1)],
                                        xn[:, 128 * q:128 * (q + 1)], ident[:, :])
                    return r
                op("tensor", ftr2)
                mt = mark("tensor", "TE")
                W("vector", "TE", mt)

                def fev2(V, j=j, half=half, bank=bank):
                    dst = xnT.rearrange("p (i t) -> p i t", t=TILE)[
                        :, 4 * half:4 * half + 4, 128 * j:128 * (j + 1)]
                    return V.tensor_copy(dst, bank[:, :].rearrange("p (i t) -> p i t", t=128))
                op("vector", fev2)
                mv = mark("vector", "V")
                W("tensor", "V", mv)
        m_pha_v = cnt["V"]

        def fsc(V):
            for j in range(NTC):
                V.scalar_tensor_tensor(qsc[:, j:j + 1], lnst[:, 12 + j:13 + j], 1.0 / 127.0,
                                       wmean[:, 0:1], ALU.mult, ALU.mult)
                r = V.scalar_tensor_tensor(qsc[:, 4 + j:5 + j], lnst[:, 12 + j:13 + j],
                                           1.0 / 127.0, wmean[:, 1:2], ALU.mult, ALU.mult)
            return r
        op("vector", fsc)
        m_qsc = mark("vector", "V")

        m_qev = 0
        for wave in range(2):
            for q in range(8):
                slot = q % 4
                wsem = "LDW%d" % slot
                W("sync", "TE", wring_last[slot])
                lv = dma("sync", wsem, wring[slot][:, :], wq_h[128 * q:128 * (q + 1), :])
                W("tensor", wsem, lv)
                if wave == 0 and q == 0:
                    W("tensor", "V", m_pha_v)
                if wave == 1 and q == 0:
                    W("tensor", "S", m_qev)

                def fqmm(T, q=q, wave=wave, slot=slot):
                    for c4 in range(4):
                        c = 4 * wave + c4
                        r = T.matmul(BK(c4)[:, :], wring[slot][:, 128 * c:128 * (c + 1)],
                                     xnT.rearrange("p (i t) -> p i t", t=TILE)[:, q, :],
                                     start=(q == 0), stop=(q == 7))
                    return r
                op("tensor", fqmm)
                wring_last[slot] = mark("tensor", "TE")
            mt_q = wring_last[7 % 4]
            W("scalar", "TE", mt_q)
            for c4 in range(4):
                c = 4 * wave + c4

                def fevq(S, c=c, c4=c4):
                    dst = qT2.rearrange("p (t h) -> p t h", h=16)[0:64, :, c]
                    S.activation(dst, BK(c4)[0:64, :], AF.Identity, bias=bqT[0:64, c:c + 1])
                    dst2 = qT2.rearrange("p (t h) -> p t h", h=8)[64:128, 0:TILE, c]
                    return S.activation(dst2, BK(c4)[64:128, :], AF.Identity,
                                        bias=bqT[64:128, c:c + 1])
                op("scalar", fevq)
            m_qev = mark("scalar", "S")
        W("tensor", "S", m_qev)
        idev = {}
        for w in range(8):
            if w >= 2:
                W("tensor", "S", idev[w - 2])

            def fidm(T, w=w):
                return T.matmul(BK(w % 2)[0:64, :], ident[64:128, 64:128],
                                qT2[64:128, 512 * w:512 * (w + 1)], start=True, stop=True)
            op("tensor", fidm)
            mt = mark("tensor", "TE")
            W("scalar", "TE", mt)

            def fidev(S, w=w):
                dst = qT2.rearrange("p (t h) -> p t h", h=16)[0:64, 64 * w:64 * (w + 1), 8:16]
                return S.activation(dst, BK(w % 2)[0:64, :].rearrange("p (t c) -> p t c", c=8),
                                    AF.Identity)
            op("scalar", fidev)
            idev[w] = mark("scalar", "S")
        m_qodd_s = idev[7]
        for piece in range(2):
            slot = piece
            W("sync", "TE", wring_last[slot])
            lv = dma("sync", "LDW%d" % slot,
                     wring[slot][:, :].rearrange("p (i c) -> p i c", c=DL),
                     wd_h.rearrange("(i p) c -> p i c", p=128)[:, 4 * piece:4 * piece + 4, :])
            W("tensor", "LDW%d" % slot, lv)
        W("tensor", "S", m_qodd_s)
        for c in range(2):
            def flmm(T, c=c):
                for q in range(8):
                    r = T.matmul(BK(c)[:, :],
                                 wring[q // 4][:, DL * (q % 4) + 128 * c:DL * (q % 4) + 128 * (c + 1)],
                                 xnT.rearrange("p (i t) -> p i t", t=TILE)[:, q, :],
                                 start=(q == 0), stop=(q == 7))
                return r
            op("tensor", flmm)
        mt_lat = mark("tensor", "TE")
        wring_last[0] = mt_lat
        wring_last[1] = mt_lat
        W("scalar", "TE", mt_lat)
        for c in range(2):
            def fevl(S, c=c):
                return S.activation(latT[:, TILE * c:TILE * (c + 1)], BK(c)[:, :], AF.Identity,
                                    bias=bdT[:, c:c + 1])
            op("scalar", fevl)
        m_lat = mark("scalar", "S")
        for q2 in range(2):
            for ch2 in range(2):
                slot = 2 * q2 + ch2
                W("sync", "TE", wring_last[slot])
                lv = dma("sync", "LDW%d" % slot, wring[slot][:, :],
                         wu_h[128 * q2:128 * (q2 + 1), 1024 * ch2:1024 * (ch2 + 1)])
                W("tensor", "LDW%d" % slot, lv)
        if it > 0:
            W("sync", "V", marks[("kw", it - 1, 15)])
        lvr = dma("sync", "LDR", wdel[:, :], wr_h[:, TILE * it:TILE * (it + 1)])
        W("vector", "LDR", lvr)

        def fwt(V):
            V.tensor_scalar(wdel[:, :], wdel[:, :], 1.0, None, ALU.add)
            V.drain()
        op("vector", fwt)
        mark("vector", "V")
        W("tensor", "S", m_lat)
        for c in range(16):
            bank = BK(2 + (c % 2))
            if c >= 2:
                W("tensor", "S", marks[("kvev", it, c - 2)])
                W("tensor", "V", marks[("kw", it, c - 2)])

            def fkv(T, c=c, bank=bank):
                for q2 in range(2):
                    r = T.matmul(bank[:, :],
                                 wring[2 * q2 + c // 8][:, 128 * (c % 8):128 * (c % 8 + 1)],
                                 latT[:, TILE * q2:TILE * (q2 + 1)],
                                 start=(q2 == 0), stop=(q2 == 1))
                return r
            op("tensor", fkv)
            mt = mark("tensor", "TE")
            W("scalar", "TE", mt)

            def fevkv(S, c=c, bank=bank):
                dst = kvT2.rearrange("p (t g) -> p t g", g=16)[64:128, :, c]
                return S.activation(dst, bank[64:128, :], AF.Identity, bias=buT[64:128, c:c + 1])
            op("scalar", fevkv)
            marks[("kvev", it, c)] = mark("scalar", "S")
            W("vector", "TE", mt)

            def fkw(V, c=c, bank=bank):
                dstk = kvT2.rearrange("p (t g) -> p t g", g=16)[0:64, :, c]
                return V.scalar_tensor_tensor(dstk, bank[0:64, :], buT[0:64, c:c + 1],
                                              wdel[:, :], ALU.add, ALU.mult)
            op("vector", fkw)
            marks[("kw", it, c)] = mark("vector", "V")
        for i4 in range(4):
            wring_last[i4] = cnt["TE"]
        m_kv = marks[("kvev", it, 15)]
        m_kw = marks[("kw", it, 15)]

        W("tensor", "S", m_kv)
        W("tensor", "V", m_kw)
        W("gpsimd", "S", m_kv)
        for half in range(2):
            if half == 1:
                W("tensor", "V", marks[("fin1", it, 0)])
            sb_emit = [None] * 8
            emit_flip = [None] * 8
            diag_m = [0] * 8
            flip_te = [0] * 8
            mask_m = [0] * 8
            norm_m = [0] * 8
            v3ev_m = [0] * 8
            exp_m = [0] * 8
            avdone = {}
            avev = {}
            for s in range(8):
                G0 = 32 * half + 4 * s
                qkb = BK(0 + (s % 2))
                v3b = BK(2 + (s % 2))
                avb = BK(4 + (s % 2))
                if s >= 2:
                    W("tensor", "V", v3ev_m[s - 2])

                def ftrv(T, G0=G0, v3b=v3b):
                    for g4 in range(4):
                        G = G0 + g4
                        r = T.transpose(v3b[:, 64 * g4:64 * (g4 + 1)],
                                        kvT2[64:128, 128 * G:128 * (G + 1)],
                                        ident[64:128, 64:128])
                    return r
                op("tensor", ftrv)
                mt_v3 = mark("tensor", "TE")
                if s >= 2:
                    W("tensor", "S", exp_m[s - 2])
                    W("tensor", "V", diag_m[s - 2])

                def fqk(T, G0=G0, qkb=qkb):
                    for g4 in range(4):
                        G = G0 + g4
                        r = T.matmul(qkb[:, 128 * g4:128 * (g4 + 1)],
                                     qT2[0:64, 128 * G:128 * (G + 1)],
                                     kvT2[0:64, 128 * G:128 * (G + 1)],
                                     start=True, stop=True)
                    return r
                op("tensor", fqk)
                mt_qk = mark("tensor", "TE")
                if s >= 1:
                    emit_flip[s - 1](norm_m[s - 1])
                if s >= 2:
                    sb_emit[s - 2](diag_m[s - 2])
                W("vector", "TE", mt_v3)

                def fevv3(V, s=s, v3b=v3b):
                    return V.tensor_copy(v3[:, 256 * s:256 * (s + 1)], v3b[:, 0:256])
                op("vector", fevv3)
                v3ev_m[s] = mark("vector", "V")
                W("scalar", "TE", mt_qk)
                if s >= 2:
                    W("scalar", "TE", flip_te[s - 2])

                def fexp(S, s=s, qkb=qkb):
                    return S.activation(esw[s % 2], qkb[:, :], AF.Exp, scale=0.125)
                op("scalar", fexp)
                exp_m[s] = mark("scalar", "S")
                W("gpsimd", "S", exp_m[s])

                def fmask(G, s=s):
                    return G.tensor_tensor(esw[s % 2], esw[s % 2], emask[:, :], ALU.mult)
                op("gpsimd", fmask)
                mask_m[s] = mark("gpsimd", "GP")
                W("vector", "GP", mask_m[s])

                def fden(V, s=s):
                    V.tensor_reduce(dnst[:, 4 * (s % 2):4 * (s % 2) + 4],
                                    esw[s % 2].rearrange("p (gl c) -> p gl c", c=128),
                                    AX.X, ALU.add)
                    V.drain()
                    V.reciprocal(dnst[:, 8 + 4 * (s % 2):12 + 4 * (s % 2)],
                                 dnst[:, 4 * (s % 2):4 * (s % 2) + 4])
                    V.drain()
                    for gl in range(4):
                        r = V.tensor_scalar(esw[s % 2][:, 128 * gl:128 * (gl + 1)],
                                            esw[s % 2][:, 128 * gl:128 * (gl + 1)],
                                            dnst[:, 8 + 4 * (s % 2) + gl:9 + 4 * (s % 2) + gl],
                                            None, ALU.mult)
                    return r
                op("vector", fden)
                norm_m[s] = mark("vector", "V")

                def mk_flip(s=s, v3b=v3b):
                    def emitf(nmark, s=s, v3b=v3b):
                        W("tensor", "V", nmark)
                        if s >= 2:
                            W("tensor", "V", diag_m[s - 2])

                        def fflip2(T, s=s):
                            qb = BK(0 + (s % 2))
                            for g4 in range(4):
                                r = T.transpose(qb[:, 128 * g4:128 * (g4 + 1)],
                                                esw[s % 2][:, 128 * g4:128 * (g4 + 1)],
                                                ident[:, :])
                            return r
                        op("tensor", fflip2)
                        mt_fl = mark("tensor", "TE")
                        flip_te[s] = mt_fl
                        W("vector", "TE", mt_fl)

                        def fevfl(V, s=s):
                            return V.tensor_copy(ediag[s % 2], BK(0 + (s % 2))[:, :])
                        op("vector", fevfl)
                        diag_m[s] = mark("vector", "V")
                    return emitf
                emit_flip[s] = mk_flip()

                def mk_emit(s=s, half=half):
                    def emit(dmark, s=s, half=half):
                        avb2 = BK(4 + (s % 2))
                        W("tensor", "V", dmark)
                        if s >= 2:
                            W("tensor", "V", avev[s - 2])

                        def fav(T, s=s, avb2=avb2):
                            for g4 in range(4):
                                r = T.matmul(avb2[64:128, 128 * g4:128 * (g4 + 1)],
                                             v3[:, 256 * s + 64 * g4:256 * s + 64 * (g4 + 1)],
                                             ediag[s % 2][:, 128 * g4:128 * (g4 + 1)],
                                             start=True, stop=True)
                            return r
                        op("tensor", fav)
                        avdone[s] = mark("tensor", "TE")
                        W("vector", "TE", avdone[s])

                        def fevav(V, s=s, avb2=avb2):
                            dst = AP(tensor=arh, offset=64 * ARW + OF_AVSB + 32 * s,
                                     ap=[[ARW, 64], [256, 16], [8, 4], [1, 8]])
                            srcb = avb2[64:128, :].rearrange("p (g j h) -> p h g j",
                                                             g=4, j=8, h=16)
                            return V.tensor_copy(dst, srcb)
                        op("vector", fevav)
                        avev[s] = mark("vector", "V")
                    return emit
                sb_emit[s] = mk_emit()
            emit_flip[7](norm_m[7])
            sb_emit[6](diag_m[6])
            sb_emit[7](diag_m[7])
            m_av_all = avev[7]

            for cp in range(8):
                ach = avch[cp % 2]
                shsem = "SH0" if cp % 2 == 0 else "SH1"
                if cp >= 2:
                    W("gpsimd", "TE", marks[("woutmm", it, half, cp - 2)])
                W("gpsimd", "V", m_av_all)
                ACOFF = 23552 + (cp % 2) * 512
                src_e = AP(tensor=arh, offset=64 * ARW + OF_AVSB + 256 * (2 * cp),
                           ap=[[ARW, 64], [1, 256]])
                dst_e = AP(tensor=arh, offset=ACOFF, ap=[[ARW, 64], [1, 256]])
                dma("gpsimd", shsem, dst_e, src_e)
                src_o = AP(tensor=arh, offset=64 * ARW + OF_AVSB + 256 * (2 * cp + 1),
                           ap=[[ARW, 64], [1, 256]])
                dst_o = AP(tensor=arh, offset=64 * ARW + ACOFF, ap=[[ARW, 64], [1, 256]])
                dma("gpsimd", shsem, dst_o, src_o)
                m_shift = cnt[shsem]
                wsem = "LDW%d" % (cp % 2)
                W("sync", "TE", wring_last[cp % 2])
                lv = dma("sync", wsem, wring[cp % 2][:, :], wo_h[128 * cp:128 * (cp + 1), :])
                W("tensor", wsem, lv)
                W("tensor", shsem, m_shift)

                def fwout(T, cp=cp, ach=ach):
                    for tc in range(2):
                        for hf in range(2):
                            r = T.matmul(PD[tc][:, 512 * hf:512 * (hf + 1)],
                                         ach[:, 128 * tc:128 * (tc + 1)],
                                         wring[cp % 2][:, 512 * hf:512 * (hf + 1)],
                                         start=(cp == 0), stop=False)
                    return r
                op("tensor", fwout)
                marks[("woutmm", it, half, cp)] = mark("tensor", "TE")
                wring_last[cp % 2] = marks[("woutmm", it, half, cp)]

            def fbo(T):
                for tc in range(2):
                    for hf in range(2):
                        r = T.matmul(PD[tc][:, 512 * hf:512 * (hf + 1)],
                                     ones[0:1, :], bo_row[0:1, 512 * hf:512 * (hf + 1)],
                                     start=False, stop=True)
                return r
            op("tensor", fbo)
            mt_wo = mark("tensor", "TE")
            for tc in range(2):
                tcg = 2 * half + tc
                xr = xring[tcg % 2]
                xsem = "LDX0" if tcg % 2 == 0 else "LDX1"
                W("sync", "V", xe_last[tcg % 2])
                lv = dma("sync", xsem, xr[:, :],
                         x_h[TILE * it + 128 * tcg: TILE * it + 128 * (tcg + 1), :])
                W("vector", xsem, lv)
                W("vector", "TE", mt_wo)

                def ffin1(V, tc=tc, tcg=tcg, xr=xr):
                    return V.scalar_tensor_tensor(fin1[:, 1024 * tcg:1024 * (tcg + 1)],
                                                  PD[tc][:, :], 1.0, xr[:, :],
                                                  ALU.bypass, ALU.add)
                op("vector", ffin1)
                xe_last[tcg % 2] = mark("vector", "V")
            marks[("fin1", it, half)] = xe_last[(2 * half + 1) % 2]
        marks[("wout_last", it, 0)] = wring_last[0]
        marks[("wout_last", it, 1)] = wring_last[1]
        m_fin1 = marks[("fin1", it, 1)]

        gu = {}
        for ob in range(8):
            gsem = "LDG0" if ob % 2 == 0 else "LDG1"
            bufs = gbufA if ob % 2 == 0 else gbufB
            if ob >= 2:
                W("sync", "TE", gu[(ob - 2, "mm")])
            elif it > 0:
                W("sync", "TE", marks[("tile_te", it - 1)])
            for mat in range(2):
                scr = tgT_h if mat == 0 else tuT_h
                dma("sync", gsem,
                    bufs[mat].rearrange("p (i c) -> p i c", c=512),
                    scr.rearrange("(i p) c -> p i c", p=128)[:, :, 512 * ob:512 * (ob + 1)])
            lv = cnt[gsem]
            W("tensor", gsem, lv)
            if ob == 0:
                W("tensor", "V", m_fin1)
            fmul_m = {}
            for tp in range(2):
                if (ob, tp) != (0, 0):
                    prev = (ob, tp - 1) if tp == 1 else (ob - 1, 1)
                    W("tensor", "V", gu[(prev[0], "ev%d" % prev[1])])

                def fgu(T, tp=tp, bufs=bufs):
                    for i in range(8):
                        for mat in range(2):
                            for tl in range(2):
                                t = 2 * tp + tl
                                r = T.matmul(PD[mat][:, 512 * tl:512 * (tl + 1)],
                                             kqT.rearrange("p (i t) -> p i t", t=TILE)[
                                                 :, i, 128 * t:128 * (t + 1)],
                                             bufs[mat].rearrange("p (i c) -> p i c", c=512)[:, i, :],
                                             start=(i == 0), stop=(i == 7))
                    return r
                op("tensor", fgu)
                gu[(ob, "mm")] = mark("tensor", "TE")
                W("scalar", "TE", gu[(ob, "mm")])
                W("vector", "TE", gu[(ob, "mm")])
                if ob == 0 and tp == 0:
                    W("scalar", "V", m_qsc)
                if ob >= 1:
                    for tl in range(2):
                        t = 2 * tp + tl
                        W("scalar", "OUTG%d" % t, marks[("gpw", it, ob - 1, t)])
                for tl in range(2):
                    t = 2 * tp + tl
                    if (ob, tp) != (0, 0):
                        W("scalar", "V", fmul_m.get((tp - 1, tl),
                                                    gu.get((ob - 1, "evt%d" % (2 + tl)), 0)))

                    def fsilu(S, t=t, tl=tl):
                        return S.activation(silustg[:, 512 * tl:512 * (tl + 1)],
                                            PD[0][:, 512 * tl:512 * (tl + 1)],
                                            AF.Silu, scale=qsc[:, t:t + 1])
                    op("scalar", fsilu)
                    ms = mark("scalar", "S")
                    W("vector", "S", ms)

                    def fmul(V, t=t, tl=tl):
                        return V.scalar_tensor_tensor(gpstg[:, 512 * t:512 * (t + 1)],
                                                      PD[1][:, 512 * tl:512 * (tl + 1)],
                                                      qsc[:, 4 + t:5 + t],
                                                      silustg[:, 512 * tl:512 * (tl + 1)],
                                                      ALU.mult, ALU.mult)
                    op("vector", fmul)
                    fmul_m[(tp, tl)] = mark("vector", "V")
                    gu[(ob, "evt%d" % t)] = fmul_m[(tp, tl)]
                gu[(ob, "ev%d" % tp)] = fmul_m[(tp, 1)]
                W("scalar", "V", fmul_m[(tp, 1)])
                for tl in range(2):
                    t = 2 * tp + tl
                    dma("scalar", "OUTG%d" % t,
                        gp_h[128 * t:128 * (t + 1), 512 * ob:512 * (ob + 1)],
                        gpstg[:, 512 * t:512 * (t + 1)])
                    marks[("gpw", it, ob, t)] = cnt["OUTG%d" % t]
            gu[(ob, "ev")] = fmul_m[(1, 1)]

        k2done = {}
        dmm_prev = None
        m_fin2 = 0
        for pair in range(2):
            for tl in range(2):
                t = 2 * pair + tl
                W("sync", "OUTG%d" % t, marks[("gpw", it, 7, t)])
                W("gpsimd", "OUTG%d" % t, marks[("gpw", it, 7, t)])
                if t > 0:
                    W("sync", "V", k2done[t - 1])
                    W("gpsimd", "V", k2done[t - 1])
                lv = dma("sync", "LDGP", gprd[:, 0:2048],
                         gp_h[128 * t:128 * (t + 1), 0:2048])
                lv2 = dma("gpsimd", "LDGQ", gprd[:, 2048:4096],
                          gp_h[128 * t:128 * (t + 1), 2048:4096])
                W("vector", "LDGP", lv)
                W("vector", "LDGQ", lv2)
                if t == 0:
                    W("vector", "TE", gu[(7, "mm")])

                def fmax2(V, t=t):
                    V.tensor_reduce(qsc[:, 12 + t:13 + t], gprd[:, :], AX.X, ALU.max,
                                    apply_absolute_value=True)
                    V.drain()
                op("vector", fmax2)

                def fs2(V, t=t):
                    V.tensor_scalar(qsc[:, 12 + t:13 + t], qsc[:, 12 + t:13 + t], 1e-5,
                                    None, ALU.max)
                    V.drain()
                    V.scalar_tensor_tensor(qsc[:, 8 + t:9 + t], qsc[:, 12 + t:13 + t],
                                           1.0 / 127.0, wmean[:, 2:3], ALU.mult, ALU.mult)
                    V.reciprocal(qsc[:, 12 + t:13 + t], qsc[:, 12 + t:13 + t])
                    V.drain()
                    return V.tensor_scalar(qsc[:, 12 + t:13 + t], qsc[:, 12 + t:13 + t], 127.0,
                                           None, ALU.mult)
                op("vector", fs2)
                mv = mark("vector", "V")
                W("scalar", "V", mv)
                for hc in range(4):
                    def fa2(S, t=t, hc=hc):
                        return S.activation(a2stg[:, 0:1024], gprd[:, 1024 * hc:1024 * (hc + 1)],
                                            AF.Identity, bias=mg[:, 4:5],
                                            scale=qsc[:, 12 + t:13 + t])
                    op("scalar", fa2)
                    ms = mark("scalar", "S")
                    W("vector", "S", ms)

                    def fk2(V, hc=hc):
                        return V.tensor_scalar(k2[:, 1024 * hc:1024 * (hc + 1)], a2stg[:, 0:1024],
                                               mg[:, 5:6], None, ALU.add)
                    op("vector", fk2)
                    mvk = mark("vector", "V")
                    if hc < 3:
                        W("scalar", "V", mvk)
                    W("tensor", "V", mvk)
                    bank = BKB(hc % 2)

                    def ftr3(T, hc=hc, bank=bank):
                        for b in range(8):
                            blk = 8 * hc + b
                            r = T.transpose(bank[:, 128 * b:128 * (b + 1)],
                                            k2[:, 128 * blk:128 * (blk + 1)], identb[:, :])
                        return r
                    op("tensor", ftr3)
                    mt = mark("tensor", "TE")
                    W("vector", "TE", mt)

                    def fev3(V, tl=tl, hc=hc, bank=bank):
                        dstt = kq2T[:, 4096 * tl + 1024 * hc:4096 * tl + 1024 * (hc + 1)]
                        return V.tensor_copy(dstt, bank[:, :])
                    op("vector", fev3)
                    mv = mark("vector", "V")
                    W("tensor", "V", mv)
                k2done[t] = cnt["V"]

            dmm = {}
            for p in range(16):
                dsem = "LDD0" if p % 2 == 0 else "LDD1"
                if p >= 2:
                    W("sync", "TE", dmm[p - 2])
                elif pair == 1:
                    W("sync", "TE", dmm_prev[14 + p])
                else:
                    W("sync", "TE", gu[(7, "mm")])
                lv = dma("sync", dsem,
                         tdring[p % 2].rearrange("p (i c) -> p i c", c=1024),
                         tdT_h.rearrange("(i p) c -> p i c", p=128)[:, 2 * p:2 * p + 2])
                W("tensor", dsem, lv)
                if p == 0:
                    W("tensor", "V", gu[(7, "ev")] if pair == 0 else m_fin2)

                def fdn(T, p=p):
                    for i2l in range(2):
                        i2 = 2 * p + i2l
                        for tl in range(2):
                            for hf in range(2):
                                r = T.matmul(PD[tl][:, 512 * hf:512 * (hf + 1)],
                                             kq2T[:, 4096 * tl + 128 * i2:
                                                  4096 * tl + 128 * (i2 + 1)],
                                             tdring[p % 2].rearrange("p (i c) -> p i c", c=1024)[
                                                 :, i2l, 512 * hf:512 * (hf + 1)],
                                             start=(i2 == 0), stop=(i2 == 31))
                    return r
                op("tensor", fdn)
                dmm[p] = mark("tensor", "TE")
            dmm_prev = dmm
            W("vector", "TE", dmm[15])
            if pair == 0:
                for tt in range(NTC):
                    W("vector", "OUTG%d" % tt, marks[("gpw", it, 7, tt)])
            for tl in range(2):
                t = 2 * pair + tl
                osem = "OUTO0" if t % 2 == 0 else "OUTO1"
                if t >= 2:
                    W("vector", osem, marks[("outd", it, t - 2)])

                def ffin2(V, t=t, tl=tl):
                    return V.scalar_tensor_tensor(fin2[t % 2][:, :], PD[tl][:, :],
                                                  qsc[:, 8 + t:9 + t],
                                                  fin1[:, 1024 * t:1024 * (t + 1)],
                                                  ALU.mult, ALU.add)
                op("vector", ffin2)
                mv = mark("vector", "V")
                W("scalar", "V", mv)
                dma("scalar", osem, out_h[TILE * it + 128 * t:TILE * it + 128 * (t + 1), :],
                    fin2[t % 2][:, :])
                marks[("outd", it, t)] = cnt[osem]
            m_fin2 = cnt["V"]

        marks[("tile_te", it)] = mark("tensor", "TE")
        marks[("tile_v", it)] = mark("vector", "V")
        marks[("tile_s", it)] = mark("scalar", "S")
        marks[("tile_dma", it)] = {ds: cnt[ds] for ds in DMASEMS}

    for ds in DMASEMS:
        W("gpsimd", ds, cnt[ds])

    @block.sync
    def _(E):
        for f in steps["sync"]:
            f(E)

    @block.tensor
    def _(E):
        for f in steps["tensor"]:
            f(E)

    @block.vector
    def _(E):
        for f in steps["vector"]:
            f(E)

    @block.scalar
    def _(E):
        for f in steps["scalar"]:
            f(E)

    @block.gpsimd
    def _(E):
        for f in steps["gpsimd"]:
            f(E)

    stack.close()
    nc._dbg_cnt = dict(cnt)
    return nc


def _mk_emask():
    m = np.zeros((128, 512), dtype=np.float32)
    for j in range(8):
        for h in range(16):
            for gl in range(4):
                for g in range(16):
                    m[16 * j + h, 128 * gl + 16 * j + g] = 1.0
    return m


def _rope_delta():
    Dh, T = 64, 8192
    try:
        import jax
        import jax.numpy as jnp
        cpu = jax.devices("cpu")[0]
        with jax.default_device(cpu):
            inv_freq = 1.0 / 10000.0 ** (jnp.arange(0, Dh, 2, dtype=jnp.float32) / Dh)
            t = jnp.arange(T, dtype=jnp.float32)
            freqs = t[:, None] * inv_freq[None, :]
            sin_f = jnp.sin(freqs)
            cos_f = jnp.cos(freqs)
        s = np.asarray(sin_f).astype(np.float64)
        c = np.asarray(cos_f).astype(np.float64)
    except Exception:
        inv_freq = (1.0 / (np.float32(10000.0) **
                           (np.arange(0, Dh, 2, dtype=np.float32) / np.float32(Dh))))
        freqs = (np.arange(T, dtype=np.float32)[:, None] *
                 inv_freq[None, :].astype(np.float32)).astype(np.float32)
        s = np.sin(freqs).astype(np.float32).astype(np.float64)
        c = np.cos(freqs).astype(np.float32).astype(np.float64)
    return (c * c + s * s) - 1.0


def kernel(**inputs):
    xin = np.ascontiguousarray(np.asarray(inputs["L0_attn_x"], dtype=np.float32))
    B, T, Dm = xin.shape
    xf = xin.reshape(B * T, Dm)
    wq = np.ascontiguousarray(np.asarray(inputs["L0_attn_Wq"], dtype=np.float32))
    bq = np.ascontiguousarray(np.asarray(inputs["L0_attn_bq"], dtype=np.float32))
    idx = np.zeros(D, dtype=np.int64)
    for c in range(8):
        for e2 in range(2):
            h = c + 8 * e2
            idx[128 * c + 64 * e2:128 * c + 64 * e2 + 64] = np.arange(64) + 64 * h
    wq_p = np.ascontiguousarray(wq[:, idx])
    bq_p = np.ascontiguousarray(bq[idx])
    import ml_dtypes
    ins = {
        "wq": wq_p, "bq": bq_p,
        "wd": np.ascontiguousarray(np.asarray(inputs["L0_attn_Wdown"], dtype=np.float32)),
        "bd": np.ascontiguousarray(np.asarray(inputs["L0_attn_bdown"], dtype=np.float32)),
        "wu": np.ascontiguousarray(np.asarray(inputs["L0_attn_Wup"], dtype=np.float32)),
        "bu": np.ascontiguousarray(np.asarray(inputs["L0_attn_bup"], dtype=np.float32)),
        "wo": np.ascontiguousarray(np.asarray(inputs["L0_attn_Wout"], dtype=np.float32)),
        "bo": np.ascontiguousarray(np.asarray(inputs["L0_attn_bout"], dtype=np.float32)),
        "wg": np.ascontiguousarray(np.asarray(inputs["L0_ffn_w_gate"], dtype=np.float32)),
        "wup": np.ascontiguousarray(np.asarray(inputs["L0_ffn_w_up"], dtype=np.float32)),
        "wdn": np.ascontiguousarray(np.asarray(inputs["L0_ffn_w_down"], dtype=np.float32)),
        "idn": np.eye(128, dtype=np.float32),
        "idnb": np.eye(128, dtype=np.float32).astype(ml_dtypes.bfloat16),
        "ones": np.ones((128, 128), dtype=np.float32),
        "emask": _mk_emask(),
    }
    wdelta = _rope_delta()
    wr_half = []
    for half in range(2):
        d32 = wdelta[4096 * half:4096 * (half + 1)].T.astype(np.float32)
        wr_half.append(np.ascontiguousarray(np.tile(d32, (2, 1))))
    nc = build_nc()
    in_maps = []
    for core in range(N_CORES):
        m = dict(ins)
        m["x"] = np.ascontiguousarray(xf[TOK * core:TOK * (core + 1)])
        m["wr"] = wr_half[core % 2]
        in_maps.append(m)
    res = run_bass_kernel_spmd(nc, in_maps, core_ids=list(range(N_CORES)))
    out = np.concatenate([np.asarray(r["out"]) for r in res.results], axis=0)
    return out.reshape(B, T, Dm).astype(np.float32)


# version 18 (final)
# speedup vs baseline: 1.1417x; 1.0087x over previous
import numpy as np
import concourse.bass as bass
import concourse.mybir as mybir
from concourse.ap import AP
from concourse.bass_utils import run_bass_kernel_spmd

F32 = mybir.dt.float32
BF16 = mybir.dt.bfloat16
AF = mybir.ActivationFunctionType
ALU = mybir.AluOpType
AX = mybir.AxisListType

N_CORES = 8
D = 1024
DL = 256
INNER = 4096
TOK = 32768 // N_CORES
TILE = 512
NT = TOK // TILE
NTC = TILE // 128
EPS = 1e-5
C23 = 1.5 * 2.0 ** 23
C52 = 1.5 * 2.0 ** 52

ENGS = ("sync", "tensor", "vector", "scalar", "gpsimd")


def build_nc():
    nc = bass.Bass("TRN2", target_bir_lowering=False)

    x_h = nc.dram_tensor("x", [TOK, D], F32, kind="ExternalInput").ap()
    wq_h = nc.dram_tensor("wq", [D, D], F32, kind="ExternalInput").ap()
    bq_h = nc.dram_tensor("bq", [D], F32, kind="ExternalInput").ap()
    wd_h = nc.dram_tensor("wd", [D, DL], F32, kind="ExternalInput").ap()
    bd_h = nc.dram_tensor("bd", [DL], F32, kind="ExternalInput").ap()
    wu_h = nc.dram_tensor("wu", [DL, 2 * D], F32, kind="ExternalInput").ap()
    bu_h = nc.dram_tensor("bu", [2 * D], F32, kind="ExternalInput").ap()
    wo_h = nc.dram_tensor("wo", [D, D], F32, kind="ExternalInput").ap()
    bo_h = nc.dram_tensor("bo", [D], F32, kind="ExternalInput").ap()
    wg_h = nc.dram_tensor("wg", [INNER, D], F32, kind="ExternalInput").ap()
    wup_h = nc.dram_tensor("wup", [INNER, D], F32, kind="ExternalInput").ap()
    wdn_h = nc.dram_tensor("wdn", [D, INNER], F32, kind="ExternalInput").ap()
    wr_h = nc.dram_tensor("wr", [64, TOK], F32, kind="ExternalInput").ap()
    idn_h = nc.dram_tensor("idn", [128, 128], F32, kind="ExternalInput").ap()
    idnb_h = nc.dram_tensor("idnb", [128, 128], BF16, kind="ExternalInput").ap()
    ones_h = nc.dram_tensor("ones", [128, 128], F32, kind="ExternalInput").ap()
    emask_h = nc.dram_tensor("emask", [128, 512], F32, kind="ExternalInput").ap()
    out_h = nc.dram_tensor("out", [TOK, D], F32, kind="ExternalOutput").ap()

    tgT_h = nc.dram_tensor("tgT", [D, INNER], BF16, kind="Internal").ap()
    tuT_h = nc.dram_tensor("tuT", [D, INNER], BF16, kind="Internal").ap()
    tdT_h = nc.dram_tensor("tdT", [INNER, D], BF16, kind="Internal").ap()
    gp_h = nc.dram_tensor("gpscr", [TILE, INNER], F32, kind="Internal").ap()

    from contextlib import ExitStack
    stack = ExitStack()
    en = stack.enter_context

    ident = en(nc.sbuf_tensor("ident", [128, 128], F32))
    identb = en(nc.sbuf_tensor("identb", [128, 128], BF16))
    ones = en(nc.sbuf_tensor("onest", [128, 128], F32))
    mg = en(nc.sbuf_tensor("mg", [128, 8], F32))
    bqT = en(nc.sbuf_tensor("bqT", [128, 8], F32))
    bdT = en(nc.sbuf_tensor("bdT", [128, 2], F32))
    buT = en(nc.sbuf_tensor("buT", [128, 16], F32))
    wmean = en(nc.sbuf_tensor("wmean", [128, 4], F32))
    swcol = en(nc.sbuf_tensor("swcol", [128, 4], F32))
    lnst = en(nc.sbuf_tensor("lnst", [128, 32], F32))
    qsc = en(nc.sbuf_tensor("qsc", [128, 16], F32))
    redc = en(nc.sbuf_tensor("redc", [128, 36], F32))
    xring = [en(nc.sbuf_tensor(f"xr{i}", [128, D], F32)) for i in range(2)]
    kqst = en(nc.sbuf_tensor("kqst", [128, D], BF16))
    kqT = en(nc.sbuf_tensor("kqT", [128, 8 * TILE], BF16))
    wring = [en(nc.sbuf_tensor(f"wring{i}", [128, D], F32)) for i in range(4)]
    emask = en(nc.sbuf_tensor("emask_s", [128, 512], F32))
    dnst = en(nc.sbuf_tensor("dnst", [128, 16], F32))
    bo_row = en(nc.sbuf_tensor("borow", [1, D], F32))
    wdel = en(nc.sbuf_tensor("wdel", [64, 512], F32))

    ARW = 26624
    ar = en(nc.sbuf_tensor("arena", [128, ARW], F32))
    arh = ar

    def A_(off, w):
        assert off + w <= ARW, (off, w)
        return ar[:, off:off + w]

    kvT2 = A_(0, 8192)
    qT2 = A_(8192, 8192)
    OF_QT = 8192
    OF_AVSB = 8192 + 4096
    xnT = A_(16384, 4096)
    fin1 = A_(16384, 4096)
    latT = A_(20480, 1024)
    xn = A_(20480, 1024)
    v3 = A_(21504, 2048)
    avch = [A_(23552 + i * 512, 512) for i in range(2)]
    esw = [A_(24576 + i * 512, 512) for i in range(2)]
    ediag = [A_(25600 + i * 512, 512) for i in range(2)]

    gprd = A_(0, 4096)
    gpstg = A_(4096, 2048)
    silustg = A_(6144, 1024)
    a2stg = A_(7168, 1024)
    fin2 = [A_(4096 + i * 1024, 1024) for i in range(2)]
    wstg = [A_(8192 + i * 2048, 2048) for i in range(2)]


    ABW = 24576
    arB = en(nc.sbuf_tensor("arenab", [128, ABW], BF16))

    def B_(off, w):
        assert off + w <= ABW, (off, w)
        return arB[:, off:off + w]

    kq2T = B_(0, 8192)
    gbufA = [B_(8192, 4096), B_(12288, 4096)]
    gbufB = [B_(16384, 4096), B_(20480, 4096)]
    tdring = [B_(16384 + i * 2048, 2048) for i in range(2)]
    k2 = B_(20480, 4096)
    tstg = B_(0, 4096)
    ttout = B_(4096, 4096)

    PD = [en(nc.psum_tensor(f"pd{i}", [128, 1024], F32)) for i in range(3)]
    PDB = en(nc.psum_tensor("pdb", [128, 2048], BF16))

    def BK(i):
        return PD[i // 2][:, 512 * (i % 2):512 * (i % 2) + 512]

    def BKB(h):
        return PDB[:, 1024 * h:1024 * (h + 1)]

    sems = {}
    DMASEMS = ("LDC", "LDS0", "LDS1", "LDX0", "LDX1",
               "LDW0", "LDW1", "LDW2", "LDW3",
               "LDG0", "LDG1", "LDGU0", "LDGU1", "LDD0", "LDD1", "LDGP", "LDGQ", "LDR",
               "OUTS", "OUTG0", "OUTG1", "OUTG2", "OUTG3",
               "OUTO0", "OUTO1", "SH0", "SH1")
    for name in DMASEMS + ("TE", "V", "S", "GP"):
        sems[name] = en(nc.semaphore(name))

    block = en(nc.Block())
    steps = {e: [] for e in ENGS}
    cnt = {n: 0 for n in sems}

    def W(eng, sem, v):
        assert v <= cnt[sem], (eng, sem, v, cnt[sem])
        if v <= 0:
            return

        def f(E, s=sem, vv=v):
            E.wait_ge(sems[s], vv)
            return None
        steps[eng].append(f)

    def dma(eng, sem, out_ap, in_ap, slow=False):
        cnt[sem] += 16

        def f(E, o=out_ap, i=in_ap, s=sem, sl=slow):
            if sl:
                E.dma_start(out=o, in_=i, allow_slow_non_contiguous=True).then_inc(sems[s], 16)
            else:
                E.dma_start(out=o, in_=i).then_inc(sems[s], 16)
        steps[eng].append(f)
        return cnt[sem]

    def mark(eng, sem):
        cnt[sem] += 1
        prev = steps[eng].pop() if steps[eng] else None

        def g(E, p=prev, s=sem):
            inst = p(E) if p is not None else None
            if inst is None:
                E.drain().then_inc(sems[s], 1)
            else:
                inst.then_inc(sems[s], 1)
        steps[eng].append(g)
        return cnt[sem]

    def op(eng, fn):
        steps[eng].append(fn)

    dma("sync", "LDC", ident[:, :], idn_h)
    dma("sync", "LDC", identb[:, :], idnb_h)
    dma("sync", "LDC", ones[:, :], ones_h)
    for c in range(8):
        dma("sync", "LDC", bqT[:, c:c + 1], bq_h[128 * c:128 * (c + 1)].rearrange("(p a) -> p a", a=1))
    for c in range(2):
        dma("sync", "LDC", bdT[:, c:c + 1], bd_h[128 * c:128 * (c + 1)].rearrange("(p a) -> p a", a=1))
    for c in range(16):
        dma("sync", "LDC", buT[:, c:c + 1], bu_h[128 * c:128 * (c + 1)].rearrange("(p a) -> p a", a=1))
    dma("sync", "LDC", bo_row[0:1, :], bo_h.rearrange("(a d) -> a d", a=1))
    dma("sync", "LDC", emask[:, :], emask_h)
    ld_consts = cnt["LDC"]

    def mkmagic(V):
        ts = V.tensor_scalar
        ts(mg[:, 0:1], ident[:, 0:1], 0.0, C52, ALU.mult, ALU.add)
        V.drain()
        ts(mg[:, 1:2], mg[:, 0:1], 1.0, -C52, ALU.add, ALU.add)
        ts(mg[:, 2:3], mg[:, 0:1], -C23, None, ALU.add)
        V.drain()
        V.tensor_mul(mg[:, 3:4], mg[:, 2:3], mg[:, 1:2])
        V.drain()
        ts(mg[:, 4:5], mg[:, 3:4], C23, None, ALU.add)
        V.drain()
        return ts(mg[:, 5:6], mg[:, 4:5], -1.0, None, ALU.mult)
    W("vector", "LDC", ld_consts)
    op("vector", mkmagic)
    m_magic = mark("vector", "V")

    tjobs = [(wg_h, INNER, D, tgT_h, 0), (wup_h, INNER, D, tuT_h, 1), (wdn_h, D, INNER, tdT_h, 2)]
    for (whh, R, C0, scrT, mi) in tjobs:
        nchunk = R // 128
        W("sync", "V", cnt["V"])
        W("sync", "S", cnt["S"])
        lastm = {}
        GRP = 2048 // C0 if C0 <= 2048 else 1
        nr = nchunk // GRP
        for r in range(nr):
            if C0 <= 2048:
                stg = wstg[r % 2][:, 0:GRP * C0]
                if r >= 2:
                    W("sync", "V", lastm[r - 2])
                src = whh.rearrange("(c p) w -> p c w", p=128)[:, GRP * r:GRP * (r + 1), :]
                dst = stg.rearrange("p (c w) -> p c w", w=C0)
            else:
                stg = gprd[:, 0:C0]
                if r >= 1:
                    W("sync", "V", lastm[r - 1])
                src = whh[128 * r:128 * (r + 1), :]
                dst = stg
            lv = dma("sync", "LDS0" if (r % 2 == 0 or C0 > 2048) else "LDS1", dst, src)
            W("vector", "LDS0" if (r % 2 == 0 or C0 > 2048) else "LDS1", lv)

            def fabs(V, stg=stg, r=r):
                return V.tensor_reduce(redc[:, r:r + 1], stg, AX.X, ALU.add,
                                       apply_absolute_value=True)
            op("vector", fabs)
            lastm[r] = mark("vector", "V")

        def fcomb(V, nr=nr):
            V.drain()
            return V.tensor_reduce(redc[:, 33:34], redc[:, 0:nr], AX.X, ALU.add)
        op("vector", fcomb)
        mv = mark("vector", "V")
        W("tensor", "V", mv)

        def fmm(T):
            return T.matmul(BK(0)[:, 0:1], ones[:, :], redc[:, 33:34], start=True, stop=True)
        op("tensor", fmm)
        mt = mark("tensor", "TE")
        W("vector", "TE", mt)
        NEL = float(R * C0)

        def fscale(V, NEL=NEL, mi=mi):
            V.tensor_scalar(redc[:, 34:35], BK(0)[:, 0:1], 1.0 / NEL, 1e-5, ALU.mult, ALU.max)
            V.drain()
            V.tensor_copy(wmean[:, mi:mi + 1], redc[:, 34:35])
            return V.reciprocal(swcol[:, mi:mi + 1], redc[:, 34:35])
        op("vector", fscale)
        mv_s = mark("vector", "V")
        W("scalar", "V", mv_s)
        W("sync", "V", mv_s)
        tpb = 0
        lastq = {}
        CW = GRP * C0
        for r in range(nr):
            if C0 <= 2048:
                stg = wstg[r % 2][:, 0:CW]
                if r >= 2:
                    W("sync", "S", lastq[r - 2])
                else:
                    W("sync", "V", mv_s)
                src = whh.rearrange("(c p) w -> p c w", p=128)[:, GRP * r:GRP * (r + 1), :]
                dst = stg.rearrange("p (c w) -> p c w", w=C0)
            else:
                stg = gprd[:, 0:C0]
                if r >= 1:
                    W("sync", "S", lastq[r - 1])
                else:
                    W("sync", "V", mv_s)
                src = whh[128 * r:128 * (r + 1), :]
                dst = stg
            lv = dma("sync", "LDS0" if (r % 2 == 0 or C0 > 2048) else "LDS1", dst, src)
            W("scalar", "LDS0" if (r % 2 == 0 or C0 > 2048) else "LDS1", lv)

            def fa(S, stg=stg, CW=CW, mi=mi):
                return S.activation(fin1[:, 0:CW], stg, AF.Identity, bias=mg[:, 4:5],
                                    scale=swcol[:, mi:mi + 1])
            op("scalar", fa)
            lastq[r] = mark("scalar", "S")
            W("vector", "S", lastq[r])
            W("vector", "OUTS", cnt["OUTS"])
            W("vector", "TE", cnt["TE"])

            def fq(V, CW=CW):
                V.tensor_scalar(tstg[:, 0:CW], fin1[:, 0:CW], mg[:, 5:6], -1.0, ALU.add, ALU.max)
                V.drain()
                return V.tensor_scalar(tstg[:, 0:CW], tstg[:, 0:CW], 1.0, None, ALU.min)
            op("vector", fq)
            mvq = mark("vector", "V")
            W("tensor", "V", mvq)
            nblk = CW // 128
            for wv in range(0, nblk, 16):
                nb = min(16, nblk - wv)

                def ftr(T, wv=wv, nb=nb):
                    for b in range(nb):
                        rr = T.transpose(PDB[:, 128 * b:128 * (b + 1)],
                                         tstg[:, 128 * (wv + b):128 * (wv + b + 1)], identb[:, :])
                    return rr
                op("tensor", ftr)
                mtt = mark("tensor", "TE")
                W("vector", "TE", mtt)

                def fev(V, wv=wv, nb=nb):
                    return V.tensor_copy(ttout[:, 128 * wv:128 * (wv + nb)], PDB[:, 0:128 * nb])
                op("vector", fev)
                mve = mark("vector", "V")
                W("tensor", "V", mve)
            W("scalar", "V", cnt["V"])
            nbc = C0 // 128
            for g in range(GRP):
                ch = GRP * r + g
                dma("scalar", "OUTS",
                    scrT.rearrange("(b p) c -> p b c", p=128)[:, 0:nbc, 128 * ch:128 * (ch + 1)],
                    ttout[:, C0 * g:C0 * (g + 1)].rearrange("p (b c) -> p b c", c=128))
    startup_out = cnt["OUTS"]

    marks = {}
    xe_last = {0: 0, 1: 0}
    wring_last = [0] * 4
    W("sync", "OUTS", startup_out)

    for it in range(NT):
        if it > 0:
            for e in ("tensor", "vector", "scalar", "gpsimd"):
                W(e, "TE", marks[("tile_te", it - 1)])
                W(e, "V", marks[("tile_v", it - 1)])
                W(e, "S", marks[("tile_s", it - 1)])
                for ds in marks[("tile_dma", it - 1)]:
                    W(e, ds, marks[("tile_dma", it - 1)][ds])
        else:
            for e in ("tensor", "scalar"):
                W(e, "V", m_magic)

        m_fqk_prev = 0
        for j in range(NTC):
            xr = xring[j % 2]
            xsem = "LDX0" if j % 2 == 0 else "LDX1"
            W("sync", "V", xe_last[j % 2])
            lv = dma("sync", xsem, xr[:, :], x_h[TILE * it + 128 * j: TILE * it + 128 * (j + 1), :])
            W("vector", xsem, lv)
            W("scalar", xsem, lv)
            if m_fqk_prev:
                W("scalar", "V", m_fqk_prev)

            def fsum(V, xr=xr):
                return V.tensor_reduce(lnst[:, 0:1], xr[:, :], AX.X, ALU.add)
            op("vector", fsum)

            def fsq(S, xr=xr):
                return S.activation(gprd[:, 0:D], xr[:, :], AF.Square, accum_out=lnst[:, 1:2])
            op("scalar", fsq)
            ms = mark("scalar", "S")
            W("vector", "S", ms)

            def fstat(V):
                ts = V.tensor_scalar
                V.drain()
                ts(lnst[:, 2:3], lnst[:, 0:1], 1.0 / D, None, ALU.mult)
                ts(lnst[:, 3:4], lnst[:, 1:2], 1.0 / D, None, ALU.mult)
                V.drain()
                V.tensor_mul(lnst[:, 4:5], lnst[:, 2:3], lnst[:, 2:3])
                V.drain()
                V.scalar_tensor_tensor(lnst[:, 4:5], lnst[:, 3:4], 1.0, lnst[:, 4:5],
                                       ALU.bypass, ALU.subtract)
                V.drain()
                return ts(lnst[:, 4:5], lnst[:, 4:5], EPS, None, ALU.add)
            op("vector", fstat)
            mv = mark("vector", "V")
            W("scalar", "V", mv)

            def fsqrt(S):
                S.activation(lnst[:, 5:6], lnst[:, 4:5], AF.Ln)
                S.drain()
                return S.activation(lnst[:, 6:7], lnst[:, 5:6], AF.Exp, scale=-0.5)
            op("scalar", fsqrt)
            ms = mark("scalar", "S")
            W("vector", "S", ms)

            def fxn(V, j=j, xr=xr):
                V.tensor_scalar(xn[:, :], xr[:, :], lnst[:, 2:3], lnst[:, 6:7],
                                ALU.subtract, ALU.mult)
                V.drain()
                V.tensor_reduce(lnst[:, 8 + j:9 + j], xn[:, :], AX.X, ALU.max,
                                apply_absolute_value=True)
                V.drain()
                V.tensor_scalar(lnst[:, 12 + j:13 + j], lnst[:, 8 + j:9 + j], 1e-5, None, ALU.max)
                V.drain()
                V.reciprocal(lnst[:, 16 + j:17 + j], lnst[:, 12 + j:13 + j])
                V.drain()
                return V.tensor_scalar(lnst[:, 16 + j:17 + j], lnst[:, 16 + j:17 + j], 127.0,
                                       None, ALU.mult)
            op("vector", fxn)
            xe_last[j % 2] = mark("vector", "V")
            W("scalar", "V", xe_last[j % 2])

            def fqa(S, j=j):
                return S.activation(gprd[:, 0:D], xn[:, :], AF.Identity,
                                    bias=mg[:, 4:5], scale=lnst[:, 16 + j:17 + j])
            op("scalar", fqa)
            ms = mark("scalar", "S")
            W("vector", "S", ms)

            def fqk(V):
                return V.tensor_scalar(kqst[:, :], gprd[:, 0:D], mg[:, 5:6], None, ALU.add)
            op("vector", fqk)
            mv = mark("vector", "V")
            m_fqk_prev = mv
            W("tensor", "V", mv)
            for half in range(2):
                bank = BKB(half)

                def ftr(T, half=half, bank=bank):
                    for b in range(4):
                        r = T.transpose(bank[:, 128 * b:128 * (b + 1)],
                                        kqst[:, 128 * (4 * half + b):128 * (4 * half + b + 1)],
                                        identb[:, :])
                    return r
                op("tensor", ftr)
                mt = mark("tensor", "TE")
                W("vector", "TE", mt)

                def fev(V, j=j, half=half, bank=bank):
                    dst = kqT.rearrange("p (i t) -> p i t", t=TILE)[
                        :, 4 * half:4 * half + 4, 128 * j:128 * (j + 1)]
                    return V.tensor_copy(dst, bank[:, 0:512].rearrange("p (i t) -> p i t", t=128))
                op("vector", fev)
                mv = mark("vector", "V")
                W("tensor", "V", mv)
            for half in range(2):
                bank = BK(4 + half)

                def ftr2(T, half=half, bank=bank):
                    for b in range(4):
                        q = 4 * half + b
                        r = T.transpose(bank[:, 128 * b:128 * (b + 1)],
                                        xn[:, 128 * q:128 * (q + 1)], ident[:, :])
                    return r
                op("tensor", ftr2)
                mt = mark("tensor", "TE")
                W("vector", "TE", mt)

                def fev2(V, j=j, half=half, bank=bank):
                    dst = xnT.rearrange("p (i t) -> p i t", t=TILE)[
                        :, 4 * half:4 * half + 4, 128 * j:128 * (j + 1)]
                    return V.tensor_copy(dst, bank[:, :].rearrange("p (i t) -> p i t", t=128))
                op("vector", fev2)
                mv = mark("vector", "V")
                W("tensor", "V", mv)
        m_pha_v = cnt["V"]

        def fsc(V):
            for j in range(NTC):
                V.scalar_tensor_tensor(qsc[:, j:j + 1], lnst[:, 12 + j:13 + j], 1.0 / 127.0,
                                       wmean[:, 0:1], ALU.mult, ALU.mult)
                r = V.scalar_tensor_tensor(qsc[:, 4 + j:5 + j], lnst[:, 12 + j:13 + j],
                                           1.0 / 127.0, wmean[:, 1:2], ALU.mult, ALU.mult)
            return r
        op("vector", fsc)
        m_qsc = mark("vector", "V")

        m_qev = 0
        for wave in range(2):
            for q in range(8):
                slot = q % 4
                wsem = "LDW%d" % slot
                W("sync", "TE", wring_last[slot])
                lv = dma("sync", wsem, wring[slot][:, :], wq_h[128 * q:128 * (q + 1), :])
                W("tensor", wsem, lv)
                if wave == 0 and q == 0:
                    W("tensor", "V", m_pha_v)
                if wave == 1 and q == 0:
                    W("tensor", "S", m_qev)

                def fqmm(T, q=q, wave=wave, slot=slot):
                    for c4 in range(4):
                        c = 4 * wave + c4
                        r = T.matmul(BK(c4)[:, :], wring[slot][:, 128 * c:128 * (c + 1)],
                                     xnT.rearrange("p (i t) -> p i t", t=TILE)[:, q, :],
                                     start=(q == 0), stop=(q == 7))
                    return r
                op("tensor", fqmm)
                wring_last[slot] = mark("tensor", "TE")
            mt_q = wring_last[7 % 4]
            W("scalar", "TE", mt_q)
            for c4 in range(4):
                c = 4 * wave + c4

                def fevq(S, c=c, c4=c4):
                    dst = qT2.rearrange("p (t h) -> p t h", h=16)[0:64, :, c]
                    S.activation(dst, BK(c4)[0:64, :], AF.Identity, bias=bqT[0:64, c:c + 1])
                    dst2 = qT2.rearrange("p (t h) -> p t h", h=8)[64:128, 0:TILE, c]
                    return S.activation(dst2, BK(c4)[64:128, :], AF.Identity,
                                        bias=bqT[64:128, c:c + 1])
                op("scalar", fevq)
            m_qev = mark("scalar", "S")
        W("tensor", "S", m_qev)
        idev = {}
        for w in range(8):
            if w >= 2:
                W("tensor", "S", idev[w - 2])

            def fidm(T, w=w):
                return T.matmul(BK(w % 2)[0:64, :], ident[64:128, 64:128],
                                qT2[64:128, 512 * w:512 * (w + 1)], start=True, stop=True)
            op("tensor", fidm)
            mt = mark("tensor", "TE")
            W("scalar", "TE", mt)

            def fidev(S, w=w):
                dst = qT2.rearrange("p (t h) -> p t h", h=16)[0:64, 64 * w:64 * (w + 1), 8:16]
                return S.activation(dst, BK(w % 2)[0:64, :].rearrange("p (t c) -> p t c", c=8),
                                    AF.Identity)
            op("scalar", fidev)
            idev[w] = mark("scalar", "S")
        m_qodd_s = idev[7]
        for piece in range(2):
            slot = piece
            W("sync", "TE", wring_last[slot])
            lv = dma("sync", "LDW%d" % slot,
                     wring[slot][:, :].rearrange("p (i c) -> p i c", c=DL),
                     wd_h.rearrange("(i p) c -> p i c", p=128)[:, 4 * piece:4 * piece + 4, :])
            W("tensor", "LDW%d" % slot, lv)
        W("tensor", "S", m_qodd_s)
        for c in range(2):
            def flmm(T, c=c):
                for q in range(8):
                    r = T.matmul(BK(c)[:, :],
                                 wring[q // 4][:, DL * (q % 4) + 128 * c:DL * (q % 4) + 128 * (c + 1)],
                                 xnT.rearrange("p (i t) -> p i t", t=TILE)[:, q, :],
                                 start=(q == 0), stop=(q == 7))
                return r
            op("tensor", flmm)
        mt_lat = mark("tensor", "TE")
        wring_last[0] = mt_lat
        wring_last[1] = mt_lat
        W("scalar", "TE", mt_lat)
        for c in range(2):
            def fevl(S, c=c):
                return S.activation(latT[:, TILE * c:TILE * (c + 1)], BK(c)[:, :], AF.Identity,
                                    bias=bdT[:, c:c + 1])
            op("scalar", fevl)
        m_lat = mark("scalar", "S")
        for q2 in range(2):
            for ch2 in range(2):
                slot = 2 * q2 + ch2
                W("sync", "TE", wring_last[slot])
                lv = dma("sync", "LDW%d" % slot, wring[slot][:, :],
                         wu_h[128 * q2:128 * (q2 + 1), 1024 * ch2:1024 * (ch2 + 1)])
                W("tensor", "LDW%d" % slot, lv)
        if it > 0:
            W("sync", "V", marks[("kw", it - 1, 15)])
        lvr = dma("sync", "LDR", wdel[:, :], wr_h[:, TILE * it:TILE * (it + 1)])
        W("vector", "LDR", lvr)

        def fwt(V):
            V.tensor_scalar(wdel[:, :], wdel[:, :], 1.0, None, ALU.add)
            V.drain()
        op("vector", fwt)
        mark("vector", "V")
        W("tensor", "S", m_lat)
        for c in range(16):
            bank = BK(2 + (c % 2))
            if c >= 2:
                W("tensor", "S", marks[("kvev", it, c - 2)])
                W("tensor", "V", marks[("kw", it, c - 2)])

            def fkv(T, c=c, bank=bank):
                for q2 in range(2):
                    r = T.matmul(bank[:, :],
                                 wring[2 * q2 + c // 8][:, 128 * (c % 8):128 * (c % 8 + 1)],
                                 latT[:, TILE * q2:TILE * (q2 + 1)],
                                 start=(q2 == 0), stop=(q2 == 1))
                return r
            op("tensor", fkv)
            mt = mark("tensor", "TE")
            W("scalar", "TE", mt)

            def fevkv(S, c=c, bank=bank):
                dst = kvT2.rearrange("p (t g) -> p t g", g=16)[64:128, :, c]
                return S.activation(dst, bank[64:128, :], AF.Identity, bias=buT[64:128, c:c + 1])
            op("scalar", fevkv)
            marks[("kvev", it, c)] = mark("scalar", "S")
            W("vector", "TE", mt)

            def fkw(V, c=c, bank=bank):
                dstk = kvT2.rearrange("p (t g) -> p t g", g=16)[0:64, :, c]
                return V.scalar_tensor_tensor(dstk, bank[0:64, :], buT[0:64, c:c + 1],
                                              wdel[:, :], ALU.add, ALU.mult)
            op("vector", fkw)
            marks[("kw", it, c)] = mark("vector", "V")
        for i4 in range(4):
            wring_last[i4] = cnt["TE"]
        m_kv = marks[("kvev", it, 15)]
        m_kw = marks[("kw", it, 15)]

        W("tensor", "S", m_kv)
        W("tensor", "V", m_kw)
        W("gpsimd", "S", m_kv)
        for half in range(2):
            if half == 1:
                W("tensor", "V", marks[("fin1", it, 0)])
            sb_emit = [None] * 8
            emit_flip = [None] * 8
            diag_m = [0] * 8
            flip_te = [0] * 8
            mask_m = [0] * 8
            norm_m = [0] * 8
            v3ev_m = [0] * 8
            exp_m = [0] * 8
            avdone = {}
            avev = {}
            for s in range(8):
                G0 = 32 * half + 4 * s
                qkb = BK(0 + (s % 2))
                v3b = BK(2 + (s % 2))
                avb = BK(4 + (s % 2))
                if s >= 2:
                    W("tensor", "V", v3ev_m[s - 2])

                def ftrv(T, G0=G0, v3b=v3b):
                    for g4 in range(4):
                        G = G0 + g4
                        r = T.transpose(v3b[:, 64 * g4:64 * (g4 + 1)],
                                        kvT2[64:128, 128 * G:128 * (G + 1)],
                                        ident[64:128, 64:128])
                    return r
                op("tensor", ftrv)
                mt_v3 = mark("tensor", "TE")
                if s >= 2:
                    W("tensor", "S", exp_m[s - 2])
                    W("tensor", "V", diag_m[s - 2])

                def fqk(T, G0=G0, qkb=qkb):
                    for g4 in range(4):
                        G = G0 + g4
                        r = T.matmul(qkb[:, 128 * g4:128 * (g4 + 1)],
                                     qT2[0:64, 128 * G:128 * (G + 1)],
                                     kvT2[0:64, 128 * G:128 * (G + 1)],
                                     start=True, stop=True)
                    return r
                op("tensor", fqk)
                mt_qk = mark("tensor", "TE")
                if s >= 1:
                    emit_flip[s - 1](norm_m[s - 1])
                if s >= 2:
                    sb_emit[s - 2](diag_m[s - 2])
                W("vector", "TE", mt_v3)

                def fevv3(V, s=s, v3b=v3b):
                    return V.tensor_copy(v3[:, 256 * s:256 * (s + 1)], v3b[:, 0:256])
                op("vector", fevv3)
                v3ev_m[s] = mark("vector", "V")
                W("scalar", "TE", mt_qk)
                if s >= 2:
                    W("scalar", "TE", flip_te[s - 2])

                def fexp(S, s=s, qkb=qkb):
                    return S.activation(esw[s % 2], qkb[:, :], AF.Exp, scale=0.125)
                op("scalar", fexp)
                exp_m[s] = mark("scalar", "S")
                W("gpsimd", "S", exp_m[s])

                def fmask(G, s=s):
                    return G.tensor_tensor(esw[s % 2], esw[s % 2], emask[:, :], ALU.mult)
                op("gpsimd", fmask)
                mask_m[s] = mark("gpsimd", "GP")
                W("vector", "GP", mask_m[s])

                def fden(V, s=s):
                    V.tensor_reduce(dnst[:, 4 * (s % 2):4 * (s % 2) + 4],
                                    esw[s % 2].rearrange("p (gl c) -> p gl c", c=128),
                                    AX.X, ALU.add)
                    V.drain()
                    V.reciprocal(dnst[:, 8 + 4 * (s % 2):12 + 4 * (s % 2)],
                                 dnst[:, 4 * (s % 2):4 * (s % 2) + 4])
                    V.drain()
                    for gl in range(4):
                        r = V.tensor_scalar(esw[s % 2][:, 128 * gl:128 * (gl + 1)],
                                            esw[s % 2][:, 128 * gl:128 * (gl + 1)],
                                            dnst[:, 8 + 4 * (s % 2) + gl:9 + 4 * (s % 2) + gl],
                                            None, ALU.mult)
                    return r
                op("vector", fden)
                norm_m[s] = mark("vector", "V")

                def mk_flip(s=s, v3b=v3b):
                    def emitf(nmark, s=s, v3b=v3b):
                        W("tensor", "V", nmark)
                        if s >= 2:
                            W("tensor", "V", diag_m[s - 2])

                        def fflip2(T, s=s):
                            qb = BK(0 + (s % 2))
                            for g4 in range(4):
                                r = T.transpose(qb[:, 128 * g4:128 * (g4 + 1)],
                                                esw[s % 2][:, 128 * g4:128 * (g4 + 1)],
                                                ident[:, :])
                            return r
                        op("tensor", fflip2)
                        mt_fl = mark("tensor", "TE")
                        flip_te[s] = mt_fl
                        W("vector", "TE", mt_fl)

                        def fevfl(V, s=s):
                            return V.tensor_copy(ediag[s % 2], BK(0 + (s % 2))[:, :])
                        op("vector", fevfl)
                        diag_m[s] = mark("vector", "V")
                    return emitf
                emit_flip[s] = mk_flip()

                def mk_emit(s=s, half=half):
                    def emit(dmark, s=s, half=half):
                        avb2 = BK(4 + (s % 2))
                        W("tensor", "V", dmark)
                        if s >= 2:
                            W("tensor", "V", avev[s - 2])

                        def fav(T, s=s, avb2=avb2):
                            for g4 in range(4):
                                r = T.matmul(avb2[64:128, 128 * g4:128 * (g4 + 1)],
                                             v3[:, 256 * s + 64 * g4:256 * s + 64 * (g4 + 1)],
                                             ediag[s % 2][:, 128 * g4:128 * (g4 + 1)],
                                             start=True, stop=True)
                            return r
                        op("tensor", fav)
                        avdone[s] = mark("tensor", "TE")
                        W("vector", "TE", avdone[s])

                        def fevav(V, s=s, avb2=avb2):
                            dst = AP(tensor=arh, offset=64 * ARW + OF_AVSB + 32 * s,
                                     ap=[[ARW, 64], [256, 16], [8, 4], [1, 8]])
                            srcb = avb2[64:128, :].rearrange("p (g j h) -> p h g j",
                                                             g=4, j=8, h=16)
                            return V.tensor_copy(dst, srcb)
                        op("vector", fevav)
                        avev[s] = mark("vector", "V")
                    return emit
                sb_emit[s] = mk_emit()
            emit_flip[7](norm_m[7])
            sb_emit[6](diag_m[6])
            sb_emit[7](diag_m[7])
            m_av_all = avev[7]

            for cp in range(8):
                ach = avch[cp % 2]
                shsem = "SH0" if cp % 2 == 0 else "SH1"
                if cp >= 2:
                    W("gpsimd", "TE", marks[("woutmm", it, half, cp - 2)])
                W("gpsimd", "V", m_av_all)
                ACOFF = 23552 + (cp % 2) * 512
                src_e = AP(tensor=arh, offset=64 * ARW + OF_AVSB + 256 * (2 * cp),
                           ap=[[ARW, 64], [1, 256]])
                dst_e = AP(tensor=arh, offset=ACOFF, ap=[[ARW, 64], [1, 256]])
                dma("gpsimd", shsem, dst_e, src_e)
                src_o = AP(tensor=arh, offset=64 * ARW + OF_AVSB + 256 * (2 * cp + 1),
                           ap=[[ARW, 64], [1, 256]])
                dst_o = AP(tensor=arh, offset=64 * ARW + ACOFF, ap=[[ARW, 64], [1, 256]])
                dma("gpsimd", shsem, dst_o, src_o)
                m_shift = cnt[shsem]
                wsem = "LDW%d" % (cp % 2)
                W("sync", "TE", wring_last[cp % 2])
                lv = dma("sync", wsem, wring[cp % 2][:, :], wo_h[128 * cp:128 * (cp + 1), :])
                W("tensor", wsem, lv)
                W("tensor", shsem, m_shift)

                def fwout(T, cp=cp, ach=ach):
                    for tc in range(2):
                        for hf in range(2):
                            r = T.matmul(PD[tc][:, 512 * hf:512 * (hf + 1)],
                                         ach[:, 128 * tc:128 * (tc + 1)],
                                         wring[cp % 2][:, 512 * hf:512 * (hf + 1)],
                                         start=(cp == 0), stop=False)
                    return r
                op("tensor", fwout)
                marks[("woutmm", it, half, cp)] = mark("tensor", "TE")
                wring_last[cp % 2] = marks[("woutmm", it, half, cp)]

            def fbo(T):
                for tc in range(2):
                    for hf in range(2):
                        r = T.matmul(PD[tc][:, 512 * hf:512 * (hf + 1)],
                                     ones[0:1, :], bo_row[0:1, 512 * hf:512 * (hf + 1)],
                                     start=False, stop=True)
                return r
            op("tensor", fbo)
            mt_wo = mark("tensor", "TE")
            for tc in range(2):
                tcg = 2 * half + tc
                xr = xring[tcg % 2]
                xsem = "LDX0" if tcg % 2 == 0 else "LDX1"
                W("sync", "V", xe_last[tcg % 2])
                lv = dma("sync", xsem, xr[:, :],
                         x_h[TILE * it + 128 * tcg: TILE * it + 128 * (tcg + 1), :])
                W("vector", xsem, lv)
                W("vector", "TE", mt_wo)

                def ffin1(V, tc=tc, tcg=tcg, xr=xr):
                    return V.scalar_tensor_tensor(fin1[:, 1024 * tcg:1024 * (tcg + 1)],
                                                  PD[tc][:, :], 1.0, xr[:, :],
                                                  ALU.bypass, ALU.add)
                op("vector", ffin1)
                xe_last[tcg % 2] = mark("vector", "V")
            marks[("fin1", it, half)] = xe_last[(2 * half + 1) % 2]
        marks[("wout_last", it, 0)] = wring_last[0]
        marks[("wout_last", it, 1)] = wring_last[1]
        m_fin1 = marks[("fin1", it, 1)]

        gu = {}
        for ob in range(8):
            gsem = "LDG0" if ob % 2 == 0 else "LDG1"
            gusem = "LDGU0" if ob % 2 == 0 else "LDGU1"
            bufs = gbufA if ob % 2 == 0 else gbufB
            if ob >= 2:
                W("sync", "TE", gu[(ob - 2, "mm")])
                W("gpsimd", "TE", gu[(ob - 2, "mm")])
            elif it > 0:
                W("sync", "TE", marks[("tile_te", it - 1)])
            lv = dma("sync", gsem,
                     bufs[0].rearrange("p (i c) -> p i c", c=512),
                     tgT_h.rearrange("(i p) c -> p i c", p=128)[:, :, 512 * ob:512 * (ob + 1)])
            lv2 = dma("gpsimd", gusem,
                      bufs[1].rearrange("p (i c) -> p i c", c=512),
                      tuT_h.rearrange("(i p) c -> p i c", p=128)[:, :, 512 * ob:512 * (ob + 1)])
            W("tensor", gsem, lv)
            W("tensor", gusem, lv2)
            if ob == 0:
                W("tensor", "V", m_fin1)
            fmul_m = {}
            for tp in range(2):
                if (ob, tp) != (0, 0):
                    prev = (ob, tp - 1) if tp == 1 else (ob - 1, 1)
                    W("tensor", "V", gu[(prev[0], "ev%d" % prev[1])])

                def fgu(T, tp=tp, bufs=bufs):
                    for i in range(8):
                        for mat in range(2):
                            for tl in range(2):
                                t = 2 * tp + tl
                                r = T.matmul(PD[mat][:, 512 * tl:512 * (tl + 1)],
                                             kqT.rearrange("p (i t) -> p i t", t=TILE)[
                                                 :, i, 128 * t:128 * (t + 1)],
                                             bufs[mat].rearrange("p (i c) -> p i c", c=512)[:, i, :],
                                             start=(i == 0), stop=(i == 7))
                    return r
                op("tensor", fgu)
                gu[(ob, "mm")] = mark("tensor", "TE")
                W("scalar", "TE", gu[(ob, "mm")])
                W("vector", "TE", gu[(ob, "mm")])
                if ob == 0 and tp == 0:
                    W("scalar", "V", m_qsc)
                if ob >= 1:
                    for tl in range(2):
                        t = 2 * tp + tl
                        W("scalar", "OUTG%d" % t, marks[("gpw", it, ob - 1, t)])
                for tl in range(2):
                    t = 2 * tp + tl
                    if (ob, tp) != (0, 0):
                        W("scalar", "V", fmul_m.get((tp - 1, tl),
                                                    gu.get((ob - 1, "evt%d" % (2 + tl)), 0)))

                    def fsilu(S, t=t, tl=tl):
                        return S.activation(silustg[:, 512 * tl:512 * (tl + 1)],
                                            PD[0][:, 512 * tl:512 * (tl + 1)],
                                            AF.Silu, scale=qsc[:, t:t + 1])
                    op("scalar", fsilu)
                    ms = mark("scalar", "S")
                    W("vector", "S", ms)

                    def fmul(V, t=t, tl=tl):
                        return V.scalar_tensor_tensor(gpstg[:, 512 * t:512 * (t + 1)],
                                                      PD[1][:, 512 * tl:512 * (tl + 1)],
                                                      qsc[:, 4 + t:5 + t],
                                                      silustg[:, 512 * tl:512 * (tl + 1)],
                                                      ALU.mult, ALU.mult)
                    op("vector", fmul)
                    fmul_m[(tp, tl)] = mark("vector", "V")
                    gu[(ob, "evt%d" % t)] = fmul_m[(tp, tl)]
                gu[(ob, "ev%d" % tp)] = fmul_m[(tp, 1)]
                W("scalar", "V", fmul_m[(tp, 1)])
                for tl in range(2):
                    t = 2 * tp + tl
                    dma("scalar", "OUTG%d" % t,
                        gp_h[128 * t:128 * (t + 1), 512 * ob:512 * (ob + 1)],
                        gpstg[:, 512 * t:512 * (t + 1)])
                    marks[("gpw", it, ob, t)] = cnt["OUTG%d" % t]
            gu[(ob, "ev")] = fmul_m[(1, 1)]

        k2done = {}
        dmm_prev = None
        m_fin2 = 0
        for pair in range(2):
            for tl in range(2):
                t = 2 * pair + tl
                W("sync", "OUTG%d" % t, marks[("gpw", it, 7, t)])
                W("gpsimd", "OUTG%d" % t, marks[("gpw", it, 7, t)])
                if t > 0:
                    W("sync", "V", k2done[t - 1])
                    W("gpsimd", "V", k2done[t - 1])
                lv = dma("sync", "LDGP", gprd[:, 0:2048],
                         gp_h[128 * t:128 * (t + 1), 0:2048])
                lv2 = dma("gpsimd", "LDGQ", gprd[:, 2048:4096],
                          gp_h[128 * t:128 * (t + 1), 2048:4096])
                W("vector", "LDGP", lv)
                W("vector", "LDGQ", lv2)
                if t == 0:
                    W("vector", "TE", gu[(7, "mm")])

                def fmax2(V, t=t):
                    V.tensor_reduce(qsc[:, 12 + t:13 + t], gprd[:, :], AX.X, ALU.max,
                                    apply_absolute_value=True)
                    V.drain()
                op("vector", fmax2)

                def fs2(V, t=t):
                    V.tensor_scalar(qsc[:, 12 + t:13 + t], qsc[:, 12 + t:13 + t], 1e-5,
                                    None, ALU.max)
                    V.drain()
                    V.scalar_tensor_tensor(qsc[:, 8 + t:9 + t], qsc[:, 12 + t:13 + t],
                                           1.0 / 127.0, wmean[:, 2:3], ALU.mult, ALU.mult)
                    V.reciprocal(qsc[:, 12 + t:13 + t], qsc[:, 12 + t:13 + t])
                    V.drain()
                    return V.tensor_scalar(qsc[:, 12 + t:13 + t], qsc[:, 12 + t:13 + t], 127.0,
                                           None, ALU.mult)
                op("vector", fs2)
                mv = mark("vector", "V")
                W("scalar", "V", mv)
                for hc in range(4):
                    def fa2(S, t=t, hc=hc):
                        return S.activation(a2stg[:, 0:1024], gprd[:, 1024 * hc:1024 * (hc + 1)],
                                            AF.Identity, bias=mg[:, 4:5],
                                            scale=qsc[:, 12 + t:13 + t])
                    op("scalar", fa2)
                    ms = mark("scalar", "S")
                    W("vector", "S", ms)

                    def fk2(V, hc=hc):
                        return V.tensor_scalar(k2[:, 1024 * hc:1024 * (hc + 1)], a2stg[:, 0:1024],
                                               mg[:, 5:6], None, ALU.add)
                    op("vector", fk2)
                    mvk = mark("vector", "V")
                    if hc < 3:
                        W("scalar", "V", mvk)
                    W("tensor", "V", mvk)
                    bank = BKB(hc % 2)

                    def ftr3(T, hc=hc, bank=bank):
                        for b in range(8):
                            blk = 8 * hc + b
                            r = T.transpose(bank[:, 128 * b:128 * (b + 1)],
                                            k2[:, 128 * blk:128 * (blk + 1)], identb[:, :])
                        return r
                    op("tensor", ftr3)
                    mt = mark("tensor", "TE")
                    W("vector", "TE", mt)

                    def fev3(V, tl=tl, hc=hc, bank=bank):
                        dstt = kq2T[:, 4096 * tl + 1024 * hc:4096 * tl + 1024 * (hc + 1)]
                        return V.tensor_copy(dstt, bank[:, :])
                    op("vector", fev3)
                    mv = mark("vector", "V")
                    W("tensor", "V", mv)
                k2done[t] = cnt["V"]

            dmm = {}
            for p in range(16):
                dsem = "LDD0" if p % 2 == 0 else "LDD1"
                if p >= 2:
                    W("sync", "TE", dmm[p - 2])
                elif pair == 1:
                    W("sync", "TE", dmm_prev[14 + p])
                else:
                    W("sync", "TE", gu[(7, "mm")])
                lv = dma("sync", dsem,
                         tdring[p % 2].rearrange("p (i c) -> p i c", c=1024),
                         tdT_h.rearrange("(i p) c -> p i c", p=128)[:, 2 * p:2 * p + 2])
                W("tensor", dsem, lv)
                if p == 0:
                    W("tensor", "V", gu[(7, "ev")] if pair == 0 else m_fin2)

                def fdn(T, p=p):
                    for i2l in range(2):
                        i2 = 2 * p + i2l
                        for tl in range(2):
                            for hf in range(2):
                                r = T.matmul(PD[tl][:, 512 * hf:512 * (hf + 1)],
                                             kq2T[:, 4096 * tl + 128 * i2:
                                                  4096 * tl + 128 * (i2 + 1)],
                                             tdring[p % 2].rearrange("p (i c) -> p i c", c=1024)[
                                                 :, i2l, 512 * hf:512 * (hf + 1)],
                                             start=(i2 == 0), stop=(i2 == 31))
                    return r
                op("tensor", fdn)
                dmm[p] = mark("tensor", "TE")
            dmm_prev = dmm
            W("vector", "TE", dmm[15])
            if pair == 0:
                for tt in range(NTC):
                    W("vector", "OUTG%d" % tt, marks[("gpw", it, 7, tt)])
            for tl in range(2):
                t = 2 * pair + tl
                osem = "OUTO0" if t % 2 == 0 else "OUTO1"
                if t >= 2:
                    W("vector", osem, marks[("outd", it, t - 2)])

                def ffin2(V, t=t, tl=tl):
                    return V.scalar_tensor_tensor(fin2[t % 2][:, :], PD[tl][:, :],
                                                  qsc[:, 8 + t:9 + t],
                                                  fin1[:, 1024 * t:1024 * (t + 1)],
                                                  ALU.mult, ALU.add)
                op("vector", ffin2)
                mv = mark("vector", "V")
                W("scalar", "V", mv)
                dma("scalar", osem, out_h[TILE * it + 128 * t:TILE * it + 128 * (t + 1), :],
                    fin2[t % 2][:, :])
                marks[("outd", it, t)] = cnt[osem]
            m_fin2 = cnt["V"]

        marks[("tile_te", it)] = mark("tensor", "TE")
        marks[("tile_v", it)] = mark("vector", "V")
        marks[("tile_s", it)] = mark("scalar", "S")
        marks[("tile_dma", it)] = {ds: cnt[ds] for ds in DMASEMS}

    for ds in DMASEMS:
        W("gpsimd", ds, cnt[ds])

    @block.sync
    def _(E):
        for f in steps["sync"]:
            f(E)

    @block.tensor
    def _(E):
        for f in steps["tensor"]:
            f(E)

    @block.vector
    def _(E):
        for f in steps["vector"]:
            f(E)

    @block.scalar
    def _(E):
        for f in steps["scalar"]:
            f(E)

    @block.gpsimd
    def _(E):
        for f in steps["gpsimd"]:
            f(E)

    stack.close()
    nc._dbg_cnt = dict(cnt)
    return nc


def _mk_emask():
    m = np.zeros((128, 512), dtype=np.float32)
    for j in range(8):
        for h in range(16):
            for gl in range(4):
                for g in range(16):
                    m[16 * j + h, 128 * gl + 16 * j + g] = 1.0
    return m


def _rope_delta():
    Dh, T = 64, 8192
    try:
        import jax
        import jax.numpy as jnp
        cpu = jax.devices("cpu")[0]
        with jax.default_device(cpu):
            inv_freq = 1.0 / 10000.0 ** (jnp.arange(0, Dh, 2, dtype=jnp.float32) / Dh)
            t = jnp.arange(T, dtype=jnp.float32)
            freqs = t[:, None] * inv_freq[None, :]
            sin_f = jnp.sin(freqs)
            cos_f = jnp.cos(freqs)
        s = np.asarray(sin_f).astype(np.float64)
        c = np.asarray(cos_f).astype(np.float64)
    except Exception:
        inv_freq = (1.0 / (np.float32(10000.0) **
                           (np.arange(0, Dh, 2, dtype=np.float32) / np.float32(Dh))))
        freqs = (np.arange(T, dtype=np.float32)[:, None] *
                 inv_freq[None, :].astype(np.float32)).astype(np.float32)
        s = np.sin(freqs).astype(np.float32).astype(np.float64)
        c = np.cos(freqs).astype(np.float32).astype(np.float64)
    return (c * c + s * s) - 1.0


def kernel(**inputs):
    xin = np.ascontiguousarray(np.asarray(inputs["L0_attn_x"], dtype=np.float32))
    B, T, Dm = xin.shape
    xf = xin.reshape(B * T, Dm)
    wq = np.ascontiguousarray(np.asarray(inputs["L0_attn_Wq"], dtype=np.float32))
    bq = np.ascontiguousarray(np.asarray(inputs["L0_attn_bq"], dtype=np.float32))
    idx = np.zeros(D, dtype=np.int64)
    for c in range(8):
        for e2 in range(2):
            h = c + 8 * e2
            idx[128 * c + 64 * e2:128 * c + 64 * e2 + 64] = np.arange(64) + 64 * h
    wq_p = np.ascontiguousarray(wq[:, idx])
    bq_p = np.ascontiguousarray(bq[idx])
    import ml_dtypes
    ins = {
        "wq": wq_p, "bq": bq_p,
        "wd": np.ascontiguousarray(np.asarray(inputs["L0_attn_Wdown"], dtype=np.float32)),
        "bd": np.ascontiguousarray(np.asarray(inputs["L0_attn_bdown"], dtype=np.float32)),
        "wu": np.ascontiguousarray(np.asarray(inputs["L0_attn_Wup"], dtype=np.float32)),
        "bu": np.ascontiguousarray(np.asarray(inputs["L0_attn_bup"], dtype=np.float32)),
        "wo": np.ascontiguousarray(np.asarray(inputs["L0_attn_Wout"], dtype=np.float32)),
        "bo": np.ascontiguousarray(np.asarray(inputs["L0_attn_bout"], dtype=np.float32)),
        "wg": np.ascontiguousarray(np.asarray(inputs["L0_ffn_w_gate"], dtype=np.float32)),
        "wup": np.ascontiguousarray(np.asarray(inputs["L0_ffn_w_up"], dtype=np.float32)),
        "wdn": np.ascontiguousarray(np.asarray(inputs["L0_ffn_w_down"], dtype=np.float32)),
        "idn": np.eye(128, dtype=np.float32),
        "idnb": np.eye(128, dtype=np.float32).astype(ml_dtypes.bfloat16),
        "ones": np.ones((128, 128), dtype=np.float32),
        "emask": _mk_emask(),
    }
    wdelta = _rope_delta()
    wr_half = []
    for half in range(2):
        d32 = wdelta[4096 * half:4096 * (half + 1)].T.astype(np.float32)
        wr_half.append(np.ascontiguousarray(np.tile(d32, (2, 1))))
    nc = build_nc()
    in_maps = []
    for core in range(N_CORES):
        m = dict(ins)
        m["x"] = np.ascontiguousarray(xf[TOK * core:TOK * (core + 1)])
        m["wr"] = wr_half[core % 2]
        in_maps.append(m)
    res = run_bass_kernel_spmd(nc, in_maps, core_ids=list(range(N_CORES)))
    out = np.concatenate([np.asarray(r["out"]) for r in res.results], axis=0)
    return out.reshape(B, T, Dm).astype(np.float32)
